# version 34
# baseline (speedup 1.0000x reference)
import sys
sys.path.insert(0, "/opt/trn_rl_repo")
from contextlib import ExitStack
import numpy as np
import concourse.bass as bass
import concourse.mybir as mybir
from concourse.bass_utils import run_bass_kernel_spmd

F32 = mybir.dt.float32
BF16 = mybir.dt.bfloat16
AF = mybir.ActivationFunctionType
ALU = mybir.AluOpType
AX = mybir.AxisListType

ENGS = ("sync", "gpsimd", "vector", "scalar", "tensor")


class Plan:

    DMA_SEMS = {"sync": ("lsa", "lsb"), "gpsimd": ("lga", "lgb"),
                "scalar": ("loa", "lob")}

    def __init__(self, nc, es):
        self.nc = nc
        self.es = es
        self.q = {e: [] for e in ENGS}
        self.h = {}
        self.cnt = {}
        self.dma_par = {e: 0 for e in self.DMA_SEMS}
        for ss in [x for pair in self.DMA_SEMS.values() for x in pair]:
            self.sem(ss)

    def sem(self, name):
        self.h[name] = self.es.enter_context(self.nc.semaphore(name))
        self.cnt[name] = 0

    def wait(self, eng, tok):
        if not tok:
            return
        sname, val = tok
        if not val:
            return
        assert 0 < val <= self.cnt[sname], (sname, val, self.cnt[sname])
        h = self.h[sname]
        self.q[eng].append(lambda E, h=h, val=val: E.wait_ge(h, val))

    def step(self, eng, fn, inc=None, k=1):
        if inc is None:
            self.q[eng].append(fn)
            return None
        h = self.h[inc]
        self.cnt[inc] += k

        def run(E, fn=fn, h=h, k=k):
            fn(E).then_inc(h, k)

        self.q[eng].append(run)
        return (inc, self.cnt[inc])

    def dma(self, eng, out, in_):
        return self.dma_raw(eng, lambda E, o=out, i=in_: E.dma_start(out=o, in_=i))

    def dma_raw(self, eng, fn):
        sname = self.DMA_SEMS[eng][self.dma_par[eng]]
        self.dma_par[eng] ^= 1
        self.wait(eng, (sname, self.cnt[sname]))
        return self.step(eng, fn, inc=sname, k=16)


def build(T=4096):
    C = T // 128
    assert C % 2 == 0 and C >= 4
    D, DC, E, I, IC = 1024, 8, 8, 512, 4

    nc = bass.Bass("TRN2", target_bir_lowering=False)

    x_d = nc.dram_tensor("x", [T, D], F32, kind="ExternalInput")
    wq_d = nc.dram_tensor("Wq", [D, D], F32, kind="ExternalInput")
    wk_d = nc.dram_tensor("Wk", [D, D], F32, kind="ExternalInput")
    wv_d = nc.dram_tensor("Wv", [D, D], F32, kind="ExternalInput")
    bq_d = nc.dram_tensor("bq2", [128, 8], F32, kind="ExternalInput")
    bk_d = nc.dram_tensor("bk2", [128, 8], F32, kind="ExternalInput")
    bv_d = nc.dram_tensor("bvrow", [1, D], F32, kind="ExternalInput")
    gw_d = nc.dram_tensor("gwT", [D, E], F32, kind="ExternalInput")
    wg_d = nc.dram_tensor("wgT", [E, D, I], F32, kind="ExternalInput")
    wu_d = nc.dram_tensor("wuT", [E, D, I], F32, kind="ExternalInput")
    wd_d = nc.dram_tensor("wdT", [E, I, D], F32, kind="ExternalInput")
    out_d = nc.dram_tensor("out", [T, D], F32, kind="ExternalOutput")
    TT = ((T // 128) // 2) * ((T // 128) // 2 + 1)
    SP_d = nc.dram_tensor("SPs", [128, TT * 256], BF16, kind="Internal")
    XNT_d = nc.dram_tensor("XNTs", [128, DC * T], BF16, kind="Internal")
    XNtok_d = nc.dram_tensor("XNtoks", [T + 1, D], BF16, kind="Internal")
    CAP = (5 * T) // 32 if (5 * T) // 32 >= 512 else T // 4
    NSLOT = 8 * CAP
    TBL_d = nc.dram_tensor("TBLs", [NSLOT, 4], mybir.dt.int32, kind="Internal")
    FFNS_d = nc.dram_tensor("FFNSs", [T + 256, D], BF16, kind="Internal")
    RD_d = nc.dram_tensor("RDs", [T, D], BF16, kind="Internal")

    es = ExitStack()
    with es:
        sb = lambda name, shape, dt: es.enter_context(nc.sbuf_tensor(name, shape, dt))
        bigA = sb("bigA", [128, DC * T], BF16)
        bigB = sb("bigB", [128, max(C * 1024, 18432)], BF16)
        wsb = sb("wsb", [128, 24576], BF16)
        q_tmp = sb("q_tmp", [128, 2048], BF16)
        xnTb = sb("xnTb", [128, 2048], BF16)
        auxf = sb("auxf", [128, 3072], F32)
        cmask = sb("cmask", [128, 128], F32)
        identf = sb("identf", [128, 128], F32)
        identb = sb("identb", [128, 128], BF16)
        iota8 = sb("iota8", [128, 8], F32)
        iotam9 = sb("iotam9", [128, 8], F32)
        ones1 = sb("ones1", [1, 128], BF16)
        bvrow = sb("bvrow_s", [1, D], BF16)
        bvb = sb("bvb", [128, D], BF16)
        onesr = sb("onesr_s", [1, 256], BF16)
        gwsb = sb("gwsb", [128, DC * E], F32)
        nm = sb("nm", [128, C], F32)
        ss = sb("ss", [128, C], F32)
        cjh = sb("cjh", [128, (C // 2) * (C // 2 + 1)], F32)
        cfP = sb("cfP", [128, C], F32)
        sel_all = sb("sel_all", [128, C], F32)
        rw_all = sb("rw_all", [128, C], F32)
        lg = sb("lg", [128, 8], F32)
        amax = sb("amax", [128, 4], F32)
        diffb = sb("diffb", [128, C], F32)
        cfb = sb("cfb", [128, C], F32)
        srow = sb("srow", [128, C], F32)
        tmps = sb("tmps", [128, 32], F32)
        epsc = sb("epsc", [128, 1], F32)
        pb = [es.enter_context(nc.psum_tensor(f"pb{i}", [128, 512], F32)) for i in range(8)]

        p = Plan(nc, es)
        for sname in ("ten", "vec", "sca", "gp"):
            p.sem(sname)
        W = p.wait
        dma = lambda eng, _sem, out, in_: p.dma(eng, out, in_)

        p.step("gpsimd", lambda E: E.iota(iota8[:], pattern=[[1, 8]], base=0,
               channel_multiplier=0, allow_small_or_imprecise_dtypes=True), inc="gp")
        p.step("gpsimd", lambda E: E.iota(iotam9[:], pattern=[[1, 8]], base=-9,
               channel_multiplier=0, allow_small_or_imprecise_dtypes=True), inc="gp")
        g0 = p.step("gpsimd", lambda E: E.memset(cmask[:], 0.0), inc="gp")
        W("gpsimd", g0)
        g0 = p.step("gpsimd", lambda E: E.affine_select(out=cmask[:], in_=cmask[:],
               pattern=[[1, 128]], compare_op=ALU.is_ge, fill=-1e30, base=0,
               channel_multiplier=-1), inc="gp")
        g0 = p.step("gpsimd", lambda E: E.memset(identf[:], 1.0), inc="gp")
        W("gpsimd", g0)
        g0 = p.step("gpsimd", lambda E: E.affine_select(out=identf[:], in_=identf[:],
               pattern=[[1, 128]], compare_op=ALU.is_equal, fill=0.0, base=0,
               channel_multiplier=-1), inc="gp")
        g0 = p.step("gpsimd", lambda E: E.memset(identb[:], 1.0), inc="gp")
        W("gpsimd", g0)
        g0 = p.step("gpsimd", lambda E: E.affine_select(out=identb[:], in_=identb[:],
               pattern=[[1, 128]], compare_op=ALU.is_equal, fill=0.0, base=0,
               channel_multiplier=-1), inc="gp")
        g_const = p.step("gpsimd", lambda E: E.memset(ones1[:], 1.0), inc="gp")

        wq_sb, wk_sb, wv_sb = wsb[:, 0:8192], wsb[:, 8192:16384], wsb[:, 16384:24576]
        g_wq = dma("gpsimd", "lg_", wq_sb.rearrange("p (k f) -> p k f", k=DC),
                   wq_d.rearrange("(k p) f -> p k f", p=128))
        g_wk = dma("gpsimd", "lg_", wk_sb.rearrange("p (k f) -> p k f", k=DC),
                   wk_d.rearrange("(k p) f -> p k f", p=128))
        g_wv = dma("gpsimd", "lg_", wv_sb.rearrange("p (k f) -> p k f", k=DC),
                   wv_d.rearrange("(k p) f -> p k f", p=128))

        l_gw = dma("sync", "ls", gwsb[:].rearrange("p (k e) -> p k e", k=DC),
                   gw_d.rearrange("(k p) e -> p k e", p=128))
        bqb = auxf[:, 1024:2048]
        bkb = auxf[:, 2048:3072]
        bqr = sb("bqr_s", [128, 8], F32)
        bkr = sb("bkr_s", [128, 8], F32)
        l_bq = dma("sync", "ls", bqr[:], bq_d[:, :])
        l_bk = dma("sync", "ls", bkr[:], bk_d[:, :])
        ones128 = bigB[:, 3072:3584].bitcast(F32)[:, 0:128]
        v_ones = p.step("vector", lambda E: E.memset(ones128[:], 1.0), inc="vec")
        W("vector", l_bq)
        W("vector", l_bk)
        W("vector", v_ones)
        v_bb = None
        for kq in range(8):
            p.step("vector", lambda E, kq=kq: E.tensor_scalar(
                out=bqb[:, kq * 128:(kq + 1) * 128], in0=ones128[:],
                scalar1=bqr[:, kq:kq + 1], scalar2=None, op0=ALU.mult), inc="vec")
            v_bb = p.step("vector", lambda E, kq=kq: E.tensor_scalar(
                out=bkb[:, kq * 128:(kq + 1) * 128], in0=ones128[:],
                scalar1=bkr[:, kq:kq + 1], scalar2=None, op0=ALU.mult), inc="vec")
        l_bv = dma("gpsimd", "lg_", bvrow[:], bv_d[:, :])

        p.step("vector", lambda E: E.memset(epsc[:], 1e-5), inc="vec")
        p.step("vector", lambda E: E.memset(nm[:], 3.0e38), inc="vec")
        v_init = p.step("vector", lambda E: E.memset(ss[:], 0.0), inc="vec")

        W("tensor", l_bv)
        W("tensor", g_const)
        t_bvb = None
        for h in range(2):
            t_bvb = p.step("tensor", lambda E, h=h: E.matmul(
                pb[6 + h][:, 0:512], ones1[0:1, :], bvrow[0:1, h * 512:(h + 1) * 512],
                start=True, stop=True), inc="ten")
        W("vector", t_bvb)
        v_bvb = None
        for h in range(2):
            v_bvb = p.step("vector", lambda E, h=h: E.tensor_copy(
                out=bvb[:, h * 512:(h + 1) * 512], in_=pb[6 + h][:, 0:512]), inc="vec")

        junk_b = bigB[:, 0:1024]

        x_buf = [bigB[:, 4096 + 2048 * i: 4096 + 2048 * (i + 1)].bitcast(F32) for i in range(2)]
        
        xnT_f = auxf[:, 0:1024]

        def tcol(py, slot):
            return tmps[:, py * 16 + slot: py * 16 + slot + 1]

        x_last_use = [0, 0]
        xn_spill = [0] * C
        q_evac = [0, 0]
        s_cast_c = [0] * C
        qk_evac = {0: 0, 1: 0, 2: 0, 3: 0}
        tr_evac_prev = 0
        lg_evac_prev = 0
        spl_xnt = [0] * C
        stats_pair_tok = [0] * (C // 2)
        sp_tok = [0] * TT
        sp_slot_prev = [0, 0, 0, 0]
        k_mm_last = 0
        exp_par = [0, 0, None]
        v_evac_tok = 0
        spl_v = [0] * C
        v_mm_last = 0
        stats_mm_last = 0
        stats_done = 0
        prev_nmu = 0
        s_cast_last = 0
        last_sexp = 0

        exp_par[2] = v_bvb

        v_xn_tok = [0] * (C + 1)
        lx_tok = [0] * (C + 1)

        def emit_load(cc):
            py = cc % 2
            if cc >= 2:
                W("sync", x_last_use[py])
                W("sync", xn_spill[cc - 2])
            lx_tok[cc] = dma("sync", "ls", x_buf[py][:], x_d[cc * 128:(cc + 1) * 128, :])

        def emit_ln(cc):
            py = cc % 2
            l_x = lx_tok[cc]
            X = x_buf[py]
            W("vector", l_x)
            v_nsum = p.step("vector", lambda E, X=X, py=py: E.reduce_sum(
                out=tcol(py, 0), in_=X[:], axis=AX.X, negate=True), inc="vec")
            W("vector", v_nsum)
            v_nmean = p.step("vector", lambda E, py=py: E.tensor_scalar(
                out=tcol(py, 1), in0=tcol(py, 0), scalar1=1.0 / D, scalar2=None,
                op0=ALU.mult), inc="vec")
            W("scalar", l_x)
            s_ssum = p.step("scalar", lambda E, X=X, py=py: E.activation(
                out=bigB[:, 1024 + py * 1024: 2048 + py * 1024], in_=X[:],
                func=AF.Square, accum_out=tcol(py, 6)), inc="sca")
            W("vector", v_nmean)
            v_m2 = p.step("vector", lambda E, py=py: E.tensor_tensor(
                out=tcol(py, 2), in0=tcol(py, 1), in1=tcol(py, 1), op=ALU.mult), inc="vec")
            W("vector", s_ssum)
            W("vector", v_m2)
            v_var = p.step("vector", lambda E, py=py: E.tensor_scalar(
                out=tcol(py, 3), in0=tcol(py, 6), scalar1=1.0 / D, scalar2=tcol(py, 2),
                op0=ALU.mult, op1=ALU.subtract), inc="vec")
            W("scalar", v_var)
            s_ln = p.step("scalar", lambda E, py=py: E.activation(
                out=tcol(py, 4), in_=tcol(py, 3), func=AF.Ln, bias=epsc[:, 0:1], scale=1.0), inc="sca")
            W("scalar", s_ln)
            s_rstd = p.step("scalar", lambda E, py=py: E.activation(
                out=tcol(py, 4), in_=tcol(py, 4), func=AF.Exp, scale=-0.5), inc="sca")
            W("vector", s_rstd)
            v_xn = p.step("vector", lambda E, X=X, py=py: E.tensor_scalar(
                out=X[:], in0=X[:], scalar1=tcol(py, 1), scalar2=tcol(py, 4),
                op0=ALU.add, op1=ALU.mult), inc="vec")
            v_xn_tok[cc] = v_xn
            W("gpsimd", v_xn)
            xn_spill[cc] = dma("gpsimd", "lg_", XNtok_d[cc * 128:(cc + 1) * 128, :], X[:])

        emit_load(0)
        emit_ln(0)
        emit_load(1)
        emit_ln(1)
        for c in range(C):
            py = c % 2
            X = x_buf[py]
            v_xn = v_xn_tok[c]

            W("tensor", v_xn)
            if c == 0:
                W("tensor", g_const)
            else:
                W("tensor", qk_evac[2])
            t_tr = None
            for k in range(DC):
                t_tr = p.step("tensor", lambda E, X=X, k=k: E.transpose(
                    pb[2 + k // 4][:, (k % 4) * 128:(k % 4) * 128 + 128],
                    X[:, k * 128:(k + 1) * 128], identf[:]), inc="ten")
            x_last_use[py] = t_tr
            if c + 2 < C:
                emit_load(c + 2)
            W("scalar", t_tr)
            v_trev = None
            for h in range(2):
                v_trev = p.step("scalar", lambda E, h=h: E.copy(
                    out=xnT_f[:, h * 512:(h + 1) * 512], in_=pb[2 + h][:, 0:512]), inc="sca")
            tr_evac_prev = v_trev
            W("scalar", v_trev)
            if c >= 2:
                W("scalar", spl_xnt[c - 2])
                W("scalar", k_mm_last)
            s_cast = p.step("scalar", lambda E, py=py: E.copy(
                out=xnTb[:, py * 1024:(py + 1) * 1024], in_=xnT_f[:]), inc="sca")
            s_cast_c[c] = s_cast
            s_cast_last = s_cast

            W("tensor", v_trev)
            if c > 0:
                W("tensor", lg_evac_prev)
            if c == 0:
                W("tensor", l_gw)
                W("tensor", v_bvb)
            t_lgm = None
            for k in range(DC):
                t_lgm = p.step("tensor", lambda E, k=k: E.matmul(
                    pb[6][:, 0:8], xnT_f[:, k * 128:(k + 1) * 128],
                    gwsb[:, k * 8:(k + 1) * 8], start=(k == 0), stop=(k == DC - 1)),
                    inc="ten")
            W("vector", t_lgm)
            v_lg = p.step("vector", lambda E: E.tensor_copy(out=lg[:], in_=pb[6][:, 0:8]),
                          inc="vec")
            lg_evac_prev = v_lg
            W("vector", v_lg)
            v_top = p.step("vector", lambda E, py=py: E.reduce_max(
                out=tcol(py, 7), in_=lg[:], axis=AX.X), inc="vec")
            W("vector", v_top)
            v_eq = p.step("vector", lambda E, py=py: E.tensor_scalar(
                out=lg[:], in0=lg[:], scalar1=tcol(py, 7), scalar2=None,
                op0=ALU.is_equal), inc="vec")
            W("vector", v_eq)
            v_t8 = p.step("vector", lambda E: E.scalar_tensor_tensor(
                out=lg[:], in0=lg[:], scalar=1.0, in1=iotam9[:], op0=ALU.mult,
                op1=ALU.mult), inc="vec")
            W("vector", v_t8)
            v_t9 = p.step("vector", lambda E: E.tensor_scalar(
                out=lg[:], in0=lg[:], scalar1=9.0, scalar2=None, op0=ALU.add), inc="vec")
            W("vector", v_t9)
            v_idx = p.step("vector", lambda E, py=py, c=c: E.tensor_reduce(
                out=sel_all[:, c:c + 1], in_=lg[:], axis=AX.X, op=ALU.min), inc="vec")
            W("scalar", v_top)
            s_e1 = p.step("scalar", lambda E, py=py: E.activation(
                out=tcol(py, 10), in_=tcol(py, 7), func=AF.Exp, scale=-1.0), inc="sca")
            W("scalar", s_e1)
            s_e2 = p.step("scalar", lambda E, py=py: E.add(
                out=tcol(py, 10), in_=tcol(py, 10), add=1.0), inc="sca")
            W("vector", s_e2)
            v_rw = p.step("vector", lambda E, py=py, c=c: E.reciprocal(
                out=rw_all[:, c:c + 1], in_=tcol(py, 10)), inc="vec")
            router_done = v_rw

            xn_bf = xnTb[:, py * 1024:(py + 1) * 1024]
            W("tensor", s_cast)
            W("tensor", v_trev)
            if c == 0:
                W("tensor", g_wk)
            t_k = None
            for qf in range(8):
                for k in range(DC):
                    t_k = p.step("tensor", lambda E, qf=qf, k=k, xn_bf=xn_bf: E.matmul(
                        pb[2 + qf // 4][:, (qf % 4) * 128:(qf % 4) * 128 + 128],
                        wk_sb[:, k * 1024 + qf * 128: k * 1024 + qf * 128 + 128],
                        xn_bf[:, k * 128:(k + 1) * 128],
                        start=(k == 0), stop=(k == DC - 1)), inc="ten")
            k_mm_last = t_k
            W("vector", t_k)
            if c == 0:
                W("vector", v_bb)
            s_ke = None
            for hb in range(2):
                s_ke = p.step("vector", lambda E, hb=hb, c=c: E.tensor_tensor(
                    out=bigA.rearrange("p (k t) -> p k t", k=DC)
                        [:, hb * 4:(hb + 1) * 4, c * 128:(c + 1) * 128],
                    in0=pb[2 + hb][:, 0:512].rearrange("p (q t) -> p q t", q=4),
                    in1=bkb[:, hb * 512:(hb + 1) * 512].rearrange("p (q t) -> p q t", q=4),
                    op=ALU.add), inc="vec")
            qk_evac[2] = qk_evac[3] = s_ke
            if c + 2 < C:
                emit_ln(c + 2)

            if c > 0:
                W("tensor", qk_evac[0])
            if c == 0:
                W("tensor", g_wq)
            t_q = None
            for qf in range(8):
                for k in range(DC):
                    t_q = p.step("tensor", lambda E, qf=qf, k=k, xn_bf=xn_bf: E.matmul(
                        pb[qf // 4][:, (qf % 4) * 128:(qf % 4) * 128 + 128],
                        wq_sb[:, k * 1024 + qf * 128: k * 1024 + qf * 128 + 128],
                        xn_bf[:, k * 128:(k + 1) * 128],
                        start=(k == 0), stop=(k == DC - 1)), inc="ten")
            W("vector", t_q)
            if c == 0:
                W("vector", v_bb)
            if c >= 2:
                W("vector", stats_pair_tok[(c - 2) // 2])
            s_qe = None
            for hb in range(2):
                s_qe = p.step("vector", lambda E, hb=hb, py=py: E.tensor_tensor(
                    out=q_tmp[:, py * 1024 + hb * 512: py * 1024 + hb * 512 + 512],
                    in0=pb[hb][:, 0:512], in1=bqb[:, hb * 512:(hb + 1) * 512],
                    op=ALU.add), inc="vec")
            qk_evac[0] = qk_evac[1] = s_qe
            q_evac[py] = s_qe
            q_mm_last = t_q
            W("gpsimd", s_cast)
            spl_xnt[c] = dma("gpsimd", "lg_",
                XNT_d.rearrange("p (k t) -> p k t", k=DC)[:, :, c * 128:(c + 1) * 128],
                xnTb.rearrange("p (pp k t) -> p pp k t", pp=2, k=DC)[:, py, :, :])

            if py == 1:
                npairs = c + 1
                m_pair = c // 2
                base = m_pair * (m_pair + 1)
                STATS_BANKS = (4, 5, 7)
                W("tensor", q_evac[0])
                W("tensor", q_evac[1])
                W("tensor", qk_evac[2])
                W("tensor", prev_nmu)
                if c == 1:
                    W("vector", v_init)
                for J in range(npairs):
                    pr = J % 3
                    bnk = pb[STATS_BANKS[pr]]
                    if exp_par[pr]:
                        W("tensor", exp_par[pr])
                    t_s = None
                    for half in range(2):
                        for k in range(DC):
                            t_s = p.step("tensor", lambda E, J=J, k=k, half=half, bnk=bnk: E.matmul(
                                bnk[:, half * 128: half * 128 + 128],
                                bigA[:, k * T + J * 128: k * T + J * 128 + 128],
                                q_tmp[:, half * 1024 + k * 128: half * 1024 + k * 128 + 128],
                                start=(k == 0), stop=(k == DC - 1)), inc="ten")
                    stats_mm_last = t_s
                    W("vector", t_s)
                    v_mask = None
                    if J == c - 1:
                        v_mask = p.step("vector", lambda E, bnk=bnk: E.tensor_tensor(
                            out=bnk[:, 0:128], in0=bnk[:, 0:128], in1=cmask[:],
                            op=ALU.add), inc="vec")
                    elif J == c:
                        v_mask = p.step("vector", lambda E, bnk=bnk: E.tensor_scalar(
                            out=bnk[:, 0:128], in0=bnk[:, 0:128], scalar1=-1e30,
                            scalar2=None, op0=ALU.add), inc="vec")
                        W("vector", v_mask)
                        v_mask = p.step("vector", lambda E, bnk=bnk: E.tensor_tensor(
                            out=bnk[:, 128:256], in0=bnk[:, 128:256], in1=cmask[:],
                            op=ALU.add), inc="vec")
                    if v_mask:
                        W("vector", v_mask)
                    v_am = p.step("vector", lambda E, bnk=bnk, pr=pr: E.reduce_max(
                        out=amax[:, pr:pr + 1], in_=bnk[:, 0:256], axis=AX.X,
                        negate=True), inc="vec")
                    W("vector", v_am)
                    v_min = p.step("vector", lambda E, J=J, pr=pr, base=base: E.tensor_scalar(
                        out=cjh[:, base + J:base + J + 1], in0=amax[:, pr:pr + 1],
                        scalar1=nm[:, J:J + 1], scalar2=None, op0=ALU.min), inc="vec")
                    last_vmin = v_min
                    spr = J % 4
                    W("scalar", v_min)
                    W("scalar", sp_slot_prev[spr])
                    s_exp = p.step("scalar", lambda E, J=J, bnk=bnk, spr=spr, base=base: E.activation(
                        out=junk_b[:, spr * 256:(spr + 1) * 256], in_=bnk[:, 0:256],
                        func=AF.Exp, bias=cjh[:, base + J:base + J + 1],
                        accum_out=srow[:, J:J + 1]), inc="sca")
                    exp_par[pr] = s_exp
                    last_sexp = s_exp
                    W("sync", s_exp)
                    sp_tok[base + J] = p.dma("sync",
                        SP_d[:, (base + J) * 256:(base + J) * 256 + 256],
                        junk_b[:, spr * 256:(spr + 1) * 256])
                    sp_slot_prev[spr] = sp_tok[base + J]
                stats_pair_tok[m_pair] = stats_mm_last
                W("vector", last_vmin)
                v_diff = p.step("vector", lambda E, npairs=npairs, base=base: E.tensor_tensor(
                    out=diffb[:, 0:npairs], in0=cjh[:, base:base + npairs],
                    in1=nm[:, 0:npairs], op=ALU.subtract), inc="vec")
                W("vector", v_diff)
                v_nmu = p.step("vector", lambda E, npairs=npairs, base=base: E.tensor_copy(
                    out=nm[:, 0:npairs], in_=cjh[:, base:base + npairs]), inc="vec")
                W("scalar", last_sexp)
                W("scalar", v_diff)
                s_cf = p.step("scalar", lambda E, npairs=npairs: E.activation(
                    out=cfb[:, 0:npairs], in_=diffb[:, 0:npairs], func=AF.Exp), inc="sca")
                sca_p1_last = s_cf
                W("vector", s_cf)
                v_ssu = p.step("vector", lambda E, npairs=npairs: E.tensor_tensor(
                    out=ss[:, 0:npairs], in0=ss[:, 0:npairs], in1=cfb[:, 0:npairs],
                    op=ALU.mult), inc="vec")
                W("vector", v_ssu)
                stats_done = p.step("vector", lambda E, npairs=npairs: E.tensor_tensor(
                    out=ss[:, 0:npairs], in0=ss[:, 0:npairs], in1=srow[:, 0:npairs],
                    op=ALU.add), inc="vec")
                prev_nmu = v_nmu

        W("scalar", stats_done)
        s_lns = p.step("scalar", lambda E: E.activation(
            out=diffb[:, 0:C], in_=ss[:, 0:C], func=AF.Ln, scale=32.0), inc="sca")
        W("vector", s_lns)
        v_nmc = p.step("vector", lambda E: E.tensor_tensor(
            out=nm[:, 0:C], in0=nm[:, 0:C], in1=diffb[:, 0:C], op=ALU.subtract), inc="vec")

        ib = auxf[:, 2048:3072]
        me_t = lambda: ib[:, 0:C]
        scan_t = ib[:, 64:64 + C]
        bo_t = ib[:, 128:128 + C]
        po_t = ib[:, 192:192 + C]
        tmp_t = ib[:, 256:256 + C]
        cnt8 = ib[:, 320:328]
        base8 = ib[:, 328:336]
        slotf = ib[:, 384:384 + C]
        tokid_i = sb("tokid_i", [128, C], mybir.dt.int32)
        slot_i = sb("slot_i", [128, C], mybir.dt.int32)
        pre4 = ib[:, 832:832 + (NSLOT // 128) * 4].bitcast(mybir.dt.int32)
        zrow = sb("zrow", [1, D], BF16)

        g_tok = p.step("gpsimd", lambda E: E.iota(tokid_i[:], pattern=[[128, C]], base=0,
               channel_multiplier=1, allow_small_or_imprecise_dtypes=True), inc="gp")
        W("gpsimd", qk_evac[2])
        g_pre = p.step("gpsimd", lambda E: E.memset(pre4[:], T), inc="gp")
        pre_s = sb("pre_s", [128, NSLOT // 128], mybir.dt.int32)
        g_ps1 = p.step("gpsimd", lambda E: E.iota(pre_s[:], pattern=[[1, NSLOT // 128]],
               base=0, channel_multiplier=NSLOT // 128,
               allow_small_or_imprecise_dtypes=True), inc="gp")
        W("vector", router_done)
        W("vector", g_ps1)
        v_ps = p.step("vector", lambda E: E.tensor_scalar(
            out=pre_s[:], in0=pre_s[:], scalar1=255, scalar2=None,
            op0=ALU.bitwise_and), inc="vec")
        W("vector", v_ps)
        v_ps = p.step("vector", lambda E: E.tensor_scalar(
            out=pre_s[:], in0=pre_s[:], scalar1=T, scalar2=None,
            op0=ALU.add), inc="vec")
        pre4v = pre4.rearrange("p (f j) -> p f j", j=4)
        W("vector", g_pre)
        W("vector", v_ps)
        v_p41 = p.step("vector", lambda E: E.tensor_copy(
            out=pre4v[:, :, 1], in_=pre_s[:]), inc="vec")
        v_p42 = p.step("vector", lambda E: E.tensor_scalar(
            out=pre4v[:, :, 2], in0=pre4v[:, :, 2], scalar1=0, scalar2=None,
            op0=ALU.bitwise_and), inc="vec")
        g_zr = p.step("gpsimd", lambda E: E.memset(zrow[:], 0.0), inc="gp")
        W("gpsimd", g_zr)
        W("gpsimd", v_p41)
        W("gpsimd", v_p42)
        l_pre1 = dma("gpsimd", "lg_", TBL_d.rearrange("(p f) j -> p (f j)", p=128),
                     pre4[:])
        l_zr = dma("gpsimd", "lg_", XNtok_d[T:T + 1, :], zrow[:])

        W("vector", router_done)
        W("vector", qk_evac[2])
        v_bo = p.step("vector", lambda E: E.memset(bo_t[:], 0.0), inc="vec")
        v_po = p.step("vector", lambda E: E.memset(po_t[:], 0.0), inc="vec")
        v_c8 = p.step("vector", lambda E: E.memset(cnt8[:], 0.0), inc="vec")
        scan_done = [None] * 8
        for e in range(8):
            me = me_t()
            W("vector", v_c8)
            v_me = p.step("vector", lambda E, e=e: E.tensor_scalar(
                out=me[:], in0=sel_all[:, 0:C], scalar1=float(e), scalar2=None,
                op0=ALU.is_equal), inc="vec")
            W("vector", v_me)
            v_sc = p.step("vector", lambda E: E.tensor_tensor_scan(
                out=scan_t[:], data0=me[:], data1=me[:], initial=0.0,
                op0=ALU.add, op1=ALU.bypass), inc="vec")
            W("vector", v_sc)
            v_cn = p.step("vector", lambda E, e=e: E.tensor_copy(
                out=cnt8[:, e:e + 1], in_=scan_t[:, C - 1:C]), inc="vec")
            W("vector", v_cn)
            v_p1 = p.step("vector", lambda E: E.tensor_tensor(
                out=tmp_t[:], in0=me[:], in1=scan_t[:], op=ALU.mult), inc="vec")
            W("vector", v_p1)
            v_po = p.step("vector", lambda E: E.tensor_tensor(
                out=po_t[:], in0=po_t[:], in1=tmp_t[:], op=ALU.add), inc="vec")
            scan_done[e] = v_po
        W("tensor", lg_evac_prev)
        W("tensor", scan_done[7])
        t_tc = p.step("tensor", lambda E: E.matmul(
            pb[6][0:8, 0:128], cnt8[:, 0:8], identf[:, 0:128],
            start=True, stop=True), inc="ten")
        W("vector", t_tc)
        v_ct = p.step("vector", lambda E: E.tensor_copy(
            out=ib[:, 448:576][0:8, :], in_=pb[6][0:8, 0:128]), inc="vec")
        W("vector", v_ct)
        v_sc8 = p.step("vector", lambda E: E.tensor_tensor_scan(
            out=ib[:, 576:704][0:8, :], data0=ib[:, 448:576][0:8, :],
            data1=ib[:, 448:576][0:8, :], initial=0.0,
            op0=ALU.add, op1=ALU.bypass), inc="vec")
        W("vector", v_sc8)
        v_ex8 = p.step("vector", lambda E: E.tensor_tensor(
            out=ib[:, 576:704][0:8, :], in0=ib[:, 576:704][0:8, :],
            in1=ib[:, 448:576][0:8, :], op=ALU.subtract), inc="vec")
        W("tensor", v_ex8)
        t_tb = p.step("tensor", lambda E: E.matmul(
            pb[6][:, 128:136], ib[:, 576:704][0:8, :], identf[0:8, 0:8],
            start=True, stop=True), inc="ten")
        W("vector", t_tb)
        v_b8 = p.step("vector", lambda E: E.tensor_copy(
            out=base8[:], in_=pb[6][:, 128:136]), inc="vec")
        lg_evac_prev = v_b8
        for e in range(8):
            me = me_t()
            W("vector", v_b8)
            v_me = p.step("vector", lambda E, e=e: E.tensor_scalar(
                out=me[:], in0=sel_all[:, 0:C], scalar1=float(e), scalar2=None,
                op0=ALU.is_equal), inc="vec")
            W("vector", v_me)
            v_t = p.step("vector", lambda E, e=e: E.tensor_scalar(
                out=tmp_t[:], in0=me[:], scalar1=base8[:, e:e + 1], scalar2=None,
                op0=ALU.mult), inc="vec")
            W("vector", v_t)
            v_bo = p.step("vector", lambda E: E.tensor_tensor(
                out=bo_t[:], in0=bo_t[:], in1=tmp_t[:], op=ALU.add), inc="vec")
        W("vector", v_bo)
        v_s1 = p.step("vector", lambda E: E.tensor_scalar(
            out=slotf[:], in0=sel_all[:, 0:C], scalar1=float(CAP), scalar2=None,
            op0=ALU.mult), inc="vec")
        W("vector", v_s1)
        v_s2 = p.step("vector", lambda E: E.tensor_tensor(
            out=slotf[:], in0=slotf[:], in1=bo_t[:], op=ALU.add), inc="vec")
        W("vector", v_s2)
        v_s3 = p.step("vector", lambda E: E.scalar_tensor_tensor(
            out=slotf[:], in0=slotf[:], scalar=-1.0, in1=po_t[:],
            op0=ALU.add, op1=ALU.add), inc="vec")
        W("vector", v_s3)
        v_s4 = p.step("vector", lambda E: E.tensor_scalar(
            out=slotf[:], in0=slotf[:], scalar1=float(NSLOT - 1), scalar2=None,
            op0=ALU.min), inc="vec")
        W("vector", v_s4)
        v_si = p.step("vector", lambda E: E.tensor_copy(out=slot_i[:], in_=slotf[:]),
                      inc="vec")
        combf = ib[:, 704:832]
        combi = combf.bitcast(mybir.dt.int32)
        combiv = combi.rearrange("p (c j) -> p c j", j=4)
        W("vector", g_tok)
        W("vector", router_done)
        v_cb0 = p.step("vector", lambda E: E.tensor_copy(
            out=combiv[:, :, 0], in_=tokid_i[:]), inc="vec")
        v_cb1 = p.step("vector", lambda E: E.tensor_copy(
            out=combiv[:, :, 1], in_=tokid_i[:]), inc="vec")
        v_cb2 = p.step("vector", lambda E: E.tensor_copy(
            out=combf.rearrange("p (c j) -> p c j", j=4)[:, :, 2],
            in_=rw_all[:, 0:C]), inc="vec")
        W("gpsimd", v_si)
        W("gpsimd", v_cb0)
        W("gpsimd", v_cb1)
        W("gpsimd", v_cb2)
        W("gpsimd", l_pre1)
        from concourse.bass_core_rust import IndirectOffsetOnAxis
        sc_tok = sc_prev = None
        for cc in range(C):
            sc_prev = sc_tok
            sc_tok = p.dma_raw("gpsimd", lambda E, cc=cc: E.indirect_dma_start(
                out=TBL_d[:, :], out_offset=IndirectOffsetOnAxis(slot_i[:, cc:cc + 1], 0),
                in_=combi[:, cc * 4:cc * 4 + 4], in_offset=None))
        scatters_done = (sc_prev, sc_tok)

        xv_use = [0, 0]
        vt_done = [0] * C
        for t in range(C):
            py = t % 2
            W("sync", spl_xnt[t])
            if t < 2:
                W("sync", k_mm_last)
                W("sync", spl_xnt[C - 2])
                W("sync", spl_xnt[C - 1])
            else:
                W("sync", xv_use[py])
            l_xv = dma("sync", "ls",
                xnTb.rearrange("p (pp k t) -> p pp k t", pp=2, k=DC)[:, py, :, :],
                XNT_d.rearrange("p (k t) -> p k t", k=DC)[:, :, t * 128:(t + 1) * 128])
            W("tensor", l_xv)
            if t == 0:
                W("tensor", g_wv)
                W("tensor", q_evac[0])
                W("tensor", q_evac[1])
            if t == 1:
                W("tensor", qk_evac[2])
            if t >= 2:
                W("tensor", vt_done[t - 2])
            bb = 2 * (t % 2)
            t_v = None
            for h in range(2):
                for k in range(DC):
                    t_v = p.step("tensor", lambda E, h=h, k=k, py=py, bb=bb: E.matmul(
                        pb[bb + h][:, 0:512],
                        xnTb[:, py * 1024 + k * 128: py * 1024 + k * 128 + 128],
                        wv_sb[:, k * 1024 + h * 512: k * 1024 + h * 512 + 512],
                        start=(k == 0), stop=(k == DC - 1)), inc="ten")
            xv_use[py] = t_v
            W("vector", t_v)
            if t == 0:
                W("vector", v_bvb)
                W("vector", sca_p1_last)
                W("vector", last_sexp)
                for _pr in range(4):
                    W("vector", sp_slot_prev[_pr])
                W("vector", xn_spill[C - 1])
                W("vector", xn_spill[C - 2])
            v_vt = None
            for h in range(2):
                v_vt = p.step("vector", lambda E, h=h, t=t, bb=bb: E.tensor_tensor(
                    out=bigB[:, t * 1024 + h * 512: t * 1024 + h * 512 + 512],
                    in0=pb[bb + h][:, 0:512], in1=bvb[:, h * 512:(h + 1) * 512],
                    op=ALU.add), inc="vec")
            vt_done[t] = v_vt
        xv_last = xv_use[(C - 1) % 2]

        pt_buf = [q_tmp[:, i * 256:(i + 1) * 256] for i in range(8)]
        rdev = [wsb[:, 1024 + 1024 * i: 1024 + 1024 * (i + 1)] for i in range(2)]

        def expert_w(par):
            base = 12288 * (1 - par)
            return (wsb[:, base: base + 4096], wsb[:, base + 4096: base + 8192],
                    wsb[:, base + 8192: base + 12288])

        W("gpsimd", xv_last)
        wg0, wu0, wd0 = expert_w(0)
        g_we = [None] * E
        wg_t = dma("gpsimd", "lg_", wg0.rearrange("p (k f) -> p k f", k=DC),
                   wg_d[0].rearrange("(k p) f -> p k f", p=128))
        wu_t = dma("gpsimd", "lg_", wu0.rearrange("p (k f) -> p k f", k=DC),
                   wu_d[0].rearrange("(k p) f -> p k f", p=128))
        wd_t = dma("gpsimd", "lg_", wd0.rearrange("p (q f) -> p q f", q=IC),
                   wd_d[0].rearrange("(q p) f -> p q f", p=128))
        g_we[0] = (wg_t, wu_t, wd_t)

        pt_used = [0] * 8
        acc_evac = [0, 0]
        rd_spill = [0] * C
        rdev_free = [0, 0]
        last_pv = 0
        s_cf = 0
        v_pt = 0
        pair_i = 0
        for g in range(C // 2):
            t0, t1 = 2 * g, 2 * g + 1
            n = t1 + 1
            base = g * (g + 1)
            W("vector", v_nmc)
            W("vector", s_cf)
            v_df2 = p.step("vector", lambda E, n=n, base=base: E.tensor_tensor(
                out=diffb[:, 0:n], in0=nm[:, 0:n], in1=cjh[:, base:base + n],
                op=ALU.subtract), inc="vec")
            W("scalar", v_df2)
            W("scalar", v_pt)
            s_cf = p.step("scalar", lambda E, n=n: E.activation(
                out=cfP[:, 0:n], in_=diffb[:, 0:n], func=AF.Exp), inc="sca")
            for J in range(n):
                pr = (pair_i + J) % 8
                qeng = "sync" if J % 2 == 0 else "scalar"
                W(qeng, pt_used[pr])
                if base + J < 8:
                    W(qeng, stats_mm_last)
                W(qeng, sp_tok[base + J])
                l_t = p.dma(qeng, pt_buf[pr][:, 0:256],
                            SP_d[:, (base + J) * 256:(base + J) * 256 + 256])
                W("vector", l_t)
                W("vector", s_cf)
                v_pt = p.step("vector", lambda E, J=J, pr=pr: E.tensor_scalar(
                    out=pt_buf[pr][:, 0:256], in0=pt_buf[pr][:, 0:256],
                    scalar1=cfP[:, J:J + 1], scalar2=None, op0=ALU.mult), inc="vec")
                W("tensor", v_pt)
                if J == 0:
                    W("tensor", vt_done[t1])
                    W("tensor", acc_evac[0])
                    W("tensor", acc_evac[1])
                for t in (t0, t1):
                    if J > t:
                        continue
                    pcol = (t - t0) * 128
                    for h in range(2):
                        last_pv = p.step("tensor", lambda E, t=t, J=J, h=h, pcol=pcol, pr=pr: E.matmul(
                            pb[(t % 2) * 2 + h][:, 0:512],
                            pt_buf[pr][:, pcol: pcol + 128],
                            bigB[:, J * 1024 + h * 512: J * 1024 + h * 512 + 512],
                            start=(J == 0), stop=(J == t)), inc="ten")
                    pt_used[pr] = last_pv
            pair_i += n
            for t in (t0, t1):
                W("vector", last_pv)
                W("vector", rdev_free[t % 2])
                v_rd = None
                for h in range(2):
                    v_rd = p.step("vector", lambda E, t=t, h=h: E.tensor_copy(
                        out=rdev[t % 2][:, h * 512:(h + 1) * 512],
                        in_=pb[(t % 2) * 2 + h][:, 0:512]), inc="vec")
                acc_evac[t % 2] = v_rd
                W("scalar", v_rd)
                rd_spill[t] = dma("scalar", "lo", RD_d[t * 128:(t + 1) * 128, :],
                                  rdev[t % 2][:])
                rdev_free[t % 2] = rd_spill[t]

        SPE = CAP // 128
        assert SPE >= 4
        S = 8 * SPE
        xg_of = lambda u: bigB[:, (u % 8) * 1024: (u % 8) * 1024 + 1024]
        xgT_of = lambda u: bigB[:, 8192 + (u % 2) * 1024: 8192 + (u % 2) * 1024 + 1024]
        out_of = lambda u: bigB[:, 10240 + (u % 2) * 1024: 10240 + (u % 2) * 1024 + 1024]
        tbl_sb = sb("tbl_sb", [128, 64], mybir.dt.int32)
        tbl_fb = tbl_sb.bitcast(F32)

        def G_of(u):
            return auxf[:, (u % 2) * 1536: (u % 2) * 1536 + 512]

        def A_of(u):
            return auxf[:, (u % 2) * 1536 + 512: (u % 2) * 1536 + 1024]

        def SIL_of(u):
            return auxf[:, (u % 2) * 1536 + 1024: (u % 2) * 1536 + 1536]

        def h_of(u):
            return xnTb[:, (u % 2) * 512: (u % 2) * 512 + 512]

        def ht_of(u):
            return xnTb[:, 1024 + (u % 2) * 512: 1024 + (u % 2) * 512 + 512]

        tb_tok = [0] * S
        xsc = [bigB[:, 12288 + 2048 * i: 12288 + 2048 * (i + 1)].bitcast(F32) for i in range(2)]
        rsc = [bigB[:, 16384 + 1024 * i: 16384 + 1024 * (i + 1)] for i in range(2)]
        p4a_done = [0] * C
        p4a_free = [0, 0]
        p4a_t = 0
        gx_tok = [0] * S
        xgT_ev = [0] * S
        t_xgT = [0] * S
        t_gu_tok = [0] * S
        v_h_tok = [0] * S
        ht_ev = [0] * S
        ht_cons = [0] * S
        out_ev = [0] * S
        out_sc = [0] * S
        gu_free = [acc_evac[0], acc_evac[1]]
        ta_free = [last_sexp, last_sexp]
        htp_free = lg_evac_prev
        down_free = exp_par[2]
        h_free = [0, 0]
        aux_free = [0, 0]
        exp_done = [0] * E
        last_acc_add = 0
        last_sc = 0

        for u in range(S + 5):
            vs = [0, 1, 2, 3, 4, 5] if u == 0 else [u + 5]
            for v in vs:
                if v >= S:
                    continue
                if v >= 16:
                    W("sync", out_sc[v - 16])
                if v < 4:
                    for sct in scatters_done:
                        W("sync", sct)
                tb_tok[v] = p.dma("sync", tbl_sb[:, (v % 16) * 4: (v % 16) * 4 + 4],
                                  TBL_d[v * 128:(v + 1) * 128, :])
            gvs = [0, 1, 2, 3, 4] if u == 0 else [u + 4]
            for v in gvs:
                if v >= S:
                    continue
                W("gpsimd", tb_tok[v])
                if v < 5:
                    W("gpsimd", last_pv)
                    W("gpsimd", l_zr)
                else:
                    W("gpsimd", t_xgT[v - 5])
                gx_tok[v] = p.dma_raw("gpsimd", lambda E, v=v: E.indirect_dma_start(
                    out=xg_of(v)[:, :], out_offset=None, in_=XNtok_d[:, :],
                    in_offset=IndirectOffsetOnAxis(
                        tbl_sb[:, (v % 16) * 4: (v % 16) * 4 + 1], 0)))

            ub = u - 3
            if 0 <= ub < S:
                W("tensor", v_h_tok[ub])
                W("tensor", htp_free)
                htp = pb[6][:, 0:512].bitcast(BF16)
                t_ht = None
                for qi in range(IC):
                    t_ht = p.step("tensor", lambda E_, qi=qi, htp=htp, ub=ub: E_.transpose(
                        htp[:, qi * 128:(qi + 1) * 128],
                        h_of(ub)[:, qi * 128:(qi + 1) * 128], identb[:]), inc="ten")
                h_free[ub % 2] = t_ht
                W("scalar", t_ht)
                if ub >= 2:
                    W("scalar", ht_cons[ub - 2])
                s_ht = p.step("scalar", lambda E_, htp=htp, ub=ub: E_.copy(
                    out=ht_of(ub), in_=htp[:, 0:512]), inc="sca")
                ht_ev[ub] = s_ht
                htp_free = s_ht

            uc = u - 4
            if 0 <= uc < S:
                ec = uc // SPE
                wde = expert_w(ec % 2)[2]
                W("tensor", ht_ev[uc])
                t_d = None
                for hh in range(2):
                    W("tensor", down_free)
                    for qi in range(IC):
                        t_d = p.step("tensor", lambda E_, qi=qi, hh=hh, wde=wde, uc=uc: E_.matmul(
                            pb[7][:, 0:512],
                            ht_of(uc)[:, qi * 128:(qi + 1) * 128],
                            wde[:, qi * 1024 + hh * 512: qi * 1024 + hh * 512 + 512],
                            start=(qi == 0), stop=(qi == IC - 1)), inc="ten")
                    W("vector", t_d)
                    if uc >= 2:
                        W("vector", out_sc[uc - 2])
                    last_acc_add = p.step("vector", lambda E_, uc=uc, hh=hh: E_.tensor_copy(
                        out=out_of(uc)[:, hh * 512:(hh + 1) * 512], in_=pb[7][:, 0:512]),
                        inc="vec")
                    down_free = last_acc_add
                ht_cons[uc] = t_d
                if uc == (ec + 1) * SPE - 1:
                    exp_done[ec] = t_d
                out_ev[uc] = last_acc_add
                W("gpsimd", last_acc_add)
                W("gpsimd", tb_tok[uc])
                out_sc[uc] = p.dma_raw("gpsimd", lambda E, uc=uc: E.indirect_dma_start(
                    out=FFNS_d[:, :], out_offset=IndirectOffsetOnAxis(
                        tbl_sb[:, (uc % 16) * 4 + 1: (uc % 16) * 4 + 2], 0),
                    in_=out_of(uc)[:, :], in_offset=None))
                last_sc = out_sc[uc]

            if u < S:
                e, t = divmod(u, SPE)
                wge, wue, wde_ = expert_w(e % 2)
                if e + 1 < E and t == 3:
                    wgn, wun, wdn = expert_w((e + 1) % 2)
                    if e == 0:
                        W("gpsimd", last_pv)
                        W("gpsimd", acc_evac[0])
                        W("gpsimd", acc_evac[1])
                        W("gpsimd", rdev_free[0])
                        W("gpsimd", rdev_free[1])
                    else:
                        W("gpsimd", exp_done[e - 1])
                    wg_t = dma("gpsimd", "lg_", wgn.rearrange("p (k f) -> p k f", k=DC),
                               wg_d[e + 1].rearrange("(k p) f -> p k f", p=128))
                    wu_t = dma("gpsimd", "lg_", wun.rearrange("p (k f) -> p k f", k=DC),
                               wu_d[e + 1].rearrange("(k p) f -> p k f", p=128))
                    wd_t = dma("gpsimd", "lg_", wdn.rearrange("p (q f) -> p q f", q=IC),
                               wd_d[e + 1].rearrange("(q p) f -> p q f", p=128))
                    g_we[e + 1] = (wg_t, wu_t, wd_t)
                W("tensor", gx_tok[u])
                W("tensor", ta_free[u % 2])
                tap = pb[4 + u % 2][:, 0:512].bitcast(BF16)
                t_tx = None
                for k in range(DC):
                    t_tx = p.step("tensor", lambda E_, k=k, tap=tap, u=u: E_.transpose(
                        tap[:, k * 128:(k + 1) * 128],
                        xg_of(u)[:, k * 128:(k + 1) * 128], identb[:]), inc="ten")
                t_xgT[u] = t_tx
                W("scalar", t_tx)
                if u >= 2:
                    W("scalar", t_gu_tok[u - 2])
                s_xe = p.step("scalar", lambda E_, u=u: E_.copy(
                    out=xgT_of(u), in_=pb[4 + u % 2][:, 0:512].bitcast(BF16)[:, 0:1024]),
                    inc="sca")
                xgT_ev[u] = s_xe
                ta_free[u % 2] = s_xe

            if p4a_t < C and u >= 2 and (u % 3 != 2 or S + 6 - u <= C - p4a_t + 2):
                tt = p4a_t
                q4 = tt % 2
                if tt < 2:
                    W("sync", last_pv)
                    W("sync", stats_mm_last)
                else:
                    W("sync", p4a_done[tt - 2])
                W("sync", rd_spill[tt])
                W("sync", xn_spill[C - 1])
                W("sync", xn_spill[C - 2])
                l1 = p.dma("sync", xsc[q4][:, :], x_d[tt * 128:(tt + 1) * 128, :])
                l2 = p.dma("sync", bigA[:, tt * 1024:(tt + 1) * 1024],
                           XNtok_d[tt * 128:(tt + 1) * 128, :])
                l3 = p.dma("sync", rsc[q4][:, :], RD_d[tt * 128:(tt + 1) * 128, :])
                W("vector", l1)
                W("vector", l2)
                v_a1 = p.step("vector", lambda E_, tt=tt, q4=q4: E_.tensor_tensor(
                    out=bigA[:, tt * 1024:(tt + 1) * 1024],
                    in0=bigA[:, tt * 1024:(tt + 1) * 1024], in1=xsc[q4][:, :],
                    op=ALU.add), inc="vec")
                W("vector", l3)
                W("vector", v_a1)
                v_a2 = p.step("vector", lambda E_, tt=tt, q4=q4: E_.tensor_tensor(
                    out=bigA[:, tt * 1024:(tt + 1) * 1024],
                    in0=bigA[:, tt * 1024:(tt + 1) * 1024], in1=rsc[q4][:, :],
                    op=ALU.add), inc="vec")
                p4a_done[tt] = v_a2
                p4a_free[q4] = v_a2
                p4a_t += 1

            ua = u - 1
            if 0 <= ua < S:
                e, t = divmod(ua, SPE)
                wge, wue, _ = expert_w(e % 2)
                sp = ua % 2
                if ua == 0:
                    for wt in g_we[0]:
                        W("tensor", wt)
                elif ua % SPE == 0:
                    for wt in g_we[ua // SPE]:
                        W("tensor", wt)
                W("tensor", xgT_ev[ua])
                W("tensor", gu_free[sp])
                t_gu = None
                for k in range(DC):
                    p.step("tensor", lambda E_, k=k, sp=sp, wge=wge, ua=ua: E_.matmul(
                        pb[sp * 2][:, 0:512],
                        xgT_of(ua)[:, k * 128:(k + 1) * 128],
                        wge[:, k * 512:(k + 1) * 512],
                        start=(k == 0), stop=(k == DC - 1)), inc="ten")
                    t_gu = p.step("tensor", lambda E_, k=k, sp=sp, wue=wue, ua=ua: E_.matmul(
                        pb[sp * 2 + 1][:, 0:512],
                        xgT_of(ua)[:, k * 128:(k + 1) * 128],
                        wue[:, k * 512:(k + 1) * 512],
                        start=(k == 0), stop=(k == DC - 1)), inc="ten")
                t_gu_tok[ua] = t_gu
                sE = tbl_fb[:, (ua % 16) * 4 + 2: (ua % 16) * 4 + 3]
                W("vector", t_gu)
                W("vector", tb_tok[ua])
                W("vector", aux_free[sp])
                v_G = p.step("vector", lambda E_, sE=sE, sp=sp, ua=ua: E_.tensor_scalar(
                    out=G_of(ua), in0=pb[sp * 2][:, 0:512], scalar1=sE, scalar2=None,
                    op0=ALU.mult), inc="vec")
                W("scalar", v_G)
                s_sil = p.step("scalar", lambda E_, ua=ua: E_.activation(
                    out=SIL_of(ua), in_=G_of(ua), func=AF.Sigmoid), inc="sca")
                W("vector", v_G)
                v_A = p.step("vector", lambda E_, sE=sE, sp=sp, ua=ua: E_.scalar_tensor_tensor(
                    out=A_of(ua), in0=pb[sp * 2 + 1][:, 0:512], scalar=sE, in1=G_of(ua),
                    op0=ALU.mult, op1=ALU.mult), inc="vec")
                gu_free[sp] = v_A
                W("vector", s_sil)
                W("vector", v_A)
                W("vector", h_free[sp])
                v_h = p.step("vector", lambda E_, ua=ua: E_.tensor_tensor(
                    out=h_of(ua), in0=A_of(ua), in1=SIL_of(ua), op=ALU.mult), inc="vec")
                v_h_tok[ua] = v_h
                aux_free[sp] = v_h
        down_evac_prev = last_acc_add

        fff = [wsb[:, 1024 * i: 1024 * (i + 1)] for i in range(2)]
        fo = [wsb[:, 20480 + 2048 * i: 20480 + 2048 * (i + 1)].bitcast(F32) for i in range(2)]
        o_reads = [0, 0]
        o_store = [0, 0]
        st_last = 0
        for t in range(C):
            tp = t % 2
            if t == 0:
                W("sync", exp_done[6])
                W("sync", exp_done[E - 1])
                W("sync", out_sc[S - 1])
                W("sync", out_sc[S - 2])
            if t >= 2:
                W("sync", o_reads[tp])
            l4 = dma("sync", "ls", fff[tp][:], FFNS_d[t * 128:(t + 1) * 128, :])
            W("vector", l4)
            W("vector", p4a_done[t])
            W("vector", o_store[tp])
            v_o3 = p.step("vector", lambda E_, tp=tp, t=t: E_.tensor_tensor(
                out=fo[tp][:], in0=bigA[:, t * 1024:(t + 1) * 1024], in1=fff[tp][:],
                op=ALU.add), inc="vec")
            o_reads[tp] = v_o3
            W("scalar", v_o3)
            st_last = dma("scalar", "lo", out_d[t * 128:(t + 1) * 128, :], fo[tp][:])
            o_store[tp] = st_last
        W("scalar", st_last)

        with nc.Block() as block:
            for eng in ENGS:
                getattr(block, eng)(lambda E, eng=eng: [f(E) for f in p.q[eng]])
    return nc


def prep_inputs(x_full, Wq, bq, Wk, bk, Wv, bv, gate_w, w_gate, w_up, w_down):
    B = x_full.shape[0]
    c = np.ascontiguousarray
    shared = {
        "Wq": c(Wq), "Wk": c(Wk), "Wv": c(Wv),
        "bq2": c(bq.reshape(8, 128).T), "bk2": c(bk.reshape(8, 128).T),
        "bvrow": c(bv.reshape(1, 1024)),
        "gwT": c(gate_w.T),
        "wgT": c(np.transpose(w_gate, (0, 2, 1))),
        "wuT": c(np.transpose(w_up, (0, 2, 1))),
        "wdT": c(np.transpose(w_down, (0, 2, 1))),
    }
    return [dict(shared, x=c(x_full[b])) for b in range(B)]


def kernel(**inputs):
    x = np.asarray(inputs["L0_attn_minibatch"])
    in_maps = prep_inputs(
        x, np.asarray(inputs["L0_attn_Wq"]), np.asarray(inputs["L0_attn_bq"]),
        np.asarray(inputs["L0_attn_Wk"]), np.asarray(inputs["L0_attn_bk"]),
        np.asarray(inputs["L0_attn_Wv"]), np.asarray(inputs["L0_attn_bv"]),
        np.asarray(inputs["L0_ffn_gate_w"]), np.asarray(inputs["L0_ffn_w_gate"]),
        np.asarray(inputs["L0_ffn_w_up"]), np.asarray(inputs["L0_ffn_w_down"]))
    nc = build(T=x.shape[1])
    res = run_bass_kernel_spmd(nc, in_maps, core_ids=list(range(x.shape[0])))
    return np.stack([r["out"] for r in res.results], axis=0)


# version 35
# speedup vs baseline: 1.0452x; 1.0452x over previous
import sys
sys.path.insert(0, "/opt/trn_rl_repo")
from contextlib import ExitStack
import numpy as np
import concourse.bass as bass
import concourse.mybir as mybir
from concourse.bass_utils import run_bass_kernel_spmd

F32 = mybir.dt.float32
BF16 = mybir.dt.bfloat16
AF = mybir.ActivationFunctionType
ALU = mybir.AluOpType
AX = mybir.AxisListType

ENGS = ("sync", "gpsimd", "vector", "scalar", "tensor")


class Plan:

    DMA_SEMS = {"sync": ("lsa", "lsb"), "gpsimd": ("lga", "lgb"),
                "scalar": ("loa", "lob")}

    def __init__(self, nc, es):
        self.nc = nc
        self.es = es
        self.q = {e: [] for e in ENGS}
        self.h = {}
        self.cnt = {}
        self.dma_par = {e: 0 for e in self.DMA_SEMS}
        for ss in [x for pair in self.DMA_SEMS.values() for x in pair]:
            self.sem(ss)

    def sem(self, name):
        self.h[name] = self.es.enter_context(self.nc.semaphore(name))
        self.cnt[name] = 0

    def wait(self, eng, tok):
        if not tok:
            return
        sname, val = tok
        if not val:
            return
        assert 0 < val <= self.cnt[sname], (sname, val, self.cnt[sname])
        h = self.h[sname]
        self.q[eng].append(lambda E, h=h, val=val: E.wait_ge(h, val))

    def step(self, eng, fn, inc=None, k=1):
        if inc is None:
            self.q[eng].append(fn)
            return None
        h = self.h[inc]
        self.cnt[inc] += k

        def run(E, fn=fn, h=h, k=k):
            fn(E).then_inc(h, k)

        self.q[eng].append(run)
        return (inc, self.cnt[inc])

    def dma(self, eng, out, in_):
        return self.dma_raw(eng, lambda E, o=out, i=in_: E.dma_start(out=o, in_=i))

    def dma_raw(self, eng, fn):
        sname = self.DMA_SEMS[eng][self.dma_par[eng]]
        self.dma_par[eng] ^= 1
        self.wait(eng, (sname, self.cnt[sname]))
        return self.step(eng, fn, inc=sname, k=16)


def build(T=4096):
    C = T // 128
    assert C % 2 == 0 and C >= 4
    D, DC, E, I, IC = 1024, 8, 8, 512, 4

    nc = bass.Bass("TRN2", target_bir_lowering=False)

    x_d = nc.dram_tensor("x", [T, D], F32, kind="ExternalInput")
    wq_d = nc.dram_tensor("Wq", [D, D], F32, kind="ExternalInput")
    wk_d = nc.dram_tensor("Wk", [D, D], F32, kind="ExternalInput")
    wv_d = nc.dram_tensor("Wv", [D, D], F32, kind="ExternalInput")
    bq_d = nc.dram_tensor("bq2", [128, 8], F32, kind="ExternalInput")
    bk_d = nc.dram_tensor("bk2", [128, 8], F32, kind="ExternalInput")
    bv_d = nc.dram_tensor("bvrow", [1, D], F32, kind="ExternalInput")
    gw_d = nc.dram_tensor("gwT", [D, E], F32, kind="ExternalInput")
    wg_d = nc.dram_tensor("wgT", [E, D, I], F32, kind="ExternalInput")
    wu_d = nc.dram_tensor("wuT", [E, D, I], F32, kind="ExternalInput")
    wd_d = nc.dram_tensor("wdT", [E, I, D], F32, kind="ExternalInput")
    out_d = nc.dram_tensor("out", [T, D], F32, kind="ExternalOutput")
    TT = ((T // 128) // 2) * ((T // 128) // 2 + 1)
    SP_d = nc.dram_tensor("SPs", [128, TT * 256], BF16, kind="Internal")
    XNT_d = nc.dram_tensor("XNTs", [128, DC * T], BF16, kind="Internal")
    XNtok_d = nc.dram_tensor("XNtoks", [T + 1, D], BF16, kind="Internal")
    CAP = (5 * T) // 32 if (5 * T) // 32 >= 512 else T // 4
    NSLOT = 8 * CAP
    TBL_d = nc.dram_tensor("TBLs", [NSLOT, 4], mybir.dt.int32, kind="Internal")
    FFNS_d = nc.dram_tensor("FFNSs", [T + 256, D], BF16, kind="Internal")
    RD_d = nc.dram_tensor("RDs", [T, D], BF16, kind="Internal")

    es = ExitStack()
    with es:
        sb = lambda name, shape, dt: es.enter_context(nc.sbuf_tensor(name, shape, dt))
        bigA = sb("bigA", [128, DC * T], BF16)
        bigB = sb("bigB", [128, max(C * 1024, 18432)], BF16)
        wsb = sb("wsb", [128, 24576], BF16)
        q_tmp = sb("q_tmp", [128, 2048], BF16)
        xnTb = sb("xnTb", [128, 2048], BF16)
        auxf = sb("auxf", [128, 3072], F32)
        cmask = sb("cmask", [128, 128], F32)
        identf = sb("identf", [128, 128], F32)
        identb = sb("identb", [128, 128], BF16)
        iota8 = sb("iota8", [128, 8], F32)
        iotam9 = sb("iotam9", [128, 8], F32)
        ones1 = sb("ones1", [1, 128], BF16)
        bvrow = sb("bvrow_s", [1, D], BF16)
        bvb = sb("bvb", [128, D], BF16)
        onesr = sb("onesr_s", [1, 256], BF16)
        gwsb = sb("gwsb", [128, DC * E], F32)
        nm = sb("nm", [128, C], F32)
        ss = sb("ss", [128, C], F32)
        cjh = sb("cjh", [128, (C // 2) * (C // 2 + 1)], F32)
        cfP = sb("cfP", [128, C], F32)
        sel_all = sb("sel_all", [128, C], F32)
        rw_all = sb("rw_all", [128, C], F32)
        lg = sb("lg", [128, 8], F32)
        amax = sb("amax", [128, 4], F32)
        diffb = sb("diffb", [128, C], F32)
        cfb = sb("cfb", [128, C], F32)
        srow = sb("srow", [128, C], F32)
        tmps = sb("tmps", [128, 32], F32)
        epsc = sb("epsc", [128, 1], F32)
        pb = [es.enter_context(nc.psum_tensor(f"pb{i}", [128, 512], F32)) for i in range(8)]

        p = Plan(nc, es)
        for sname in ("ten", "vec", "sca", "gp"):
            p.sem(sname)
        W = p.wait
        dma = lambda eng, _sem, out, in_: p.dma(eng, out, in_)

        p.step("gpsimd", lambda E: E.iota(iota8[:], pattern=[[1, 8]], base=0,
               channel_multiplier=0, allow_small_or_imprecise_dtypes=True), inc="gp")
        p.step("gpsimd", lambda E: E.iota(iotam9[:], pattern=[[1, 8]], base=-9,
               channel_multiplier=0, allow_small_or_imprecise_dtypes=True), inc="gp")
        g0 = p.step("gpsimd", lambda E: E.memset(cmask[:], 0.0), inc="gp")
        W("gpsimd", g0)
        g0 = p.step("gpsimd", lambda E: E.affine_select(out=cmask[:], in_=cmask[:],
               pattern=[[1, 128]], compare_op=ALU.is_ge, fill=-1e30, base=0,
               channel_multiplier=-1), inc="gp")
        g0 = p.step("gpsimd", lambda E: E.memset(identf[:], 1.0), inc="gp")
        W("gpsimd", g0)
        g0 = p.step("gpsimd", lambda E: E.affine_select(out=identf[:], in_=identf[:],
               pattern=[[1, 128]], compare_op=ALU.is_equal, fill=0.0, base=0,
               channel_multiplier=-1), inc="gp")
        g0 = p.step("gpsimd", lambda E: E.memset(identb[:], 1.0), inc="gp")
        W("gpsimd", g0)
        g0 = p.step("gpsimd", lambda E: E.affine_select(out=identb[:], in_=identb[:],
               pattern=[[1, 128]], compare_op=ALU.is_equal, fill=0.0, base=0,
               channel_multiplier=-1), inc="gp")
        g_const = p.step("gpsimd", lambda E: E.memset(ones1[:], 1.0), inc="gp")

        wq_sb, wk_sb, wv_sb = wsb[:, 0:8192], wsb[:, 8192:16384], wsb[:, 16384:24576]
        g_wq = dma("gpsimd", "lg_", wq_sb.rearrange("p (k f) -> p k f", k=DC),
                   wq_d.rearrange("(k p) f -> p k f", p=128))
        g_wk = dma("gpsimd", "lg_", wk_sb.rearrange("p (k f) -> p k f", k=DC),
                   wk_d.rearrange("(k p) f -> p k f", p=128))
        g_wv = dma("gpsimd", "lg_", wv_sb.rearrange("p (k f) -> p k f", k=DC),
                   wv_d.rearrange("(k p) f -> p k f", p=128))

        l_gw = dma("sync", "ls", gwsb[:].rearrange("p (k e) -> p k e", k=DC),
                   gw_d.rearrange("(k p) e -> p k e", p=128))
        bqb = auxf[:, 1024:2048]
        bkb = auxf[:, 2048:3072]
        bqr = sb("bqr_s", [128, 8], F32)
        bkr = sb("bkr_s", [128, 8], F32)
        l_bq = dma("sync", "ls", bqr[:], bq_d[:, :])
        l_bk = dma("sync", "ls", bkr[:], bk_d[:, :])
        ones128 = bigB[:, 3072:3584].bitcast(F32)[:, 0:128]
        v_ones = p.step("vector", lambda E: E.memset(ones128[:], 1.0), inc="vec")
        W("vector", l_bq)
        W("vector", l_bk)
        W("vector", v_ones)
        v_bb = None
        for kq in range(8):
            p.step("vector", lambda E, kq=kq: E.tensor_scalar(
                out=bqb[:, kq * 128:(kq + 1) * 128], in0=ones128[:],
                scalar1=bqr[:, kq:kq + 1], scalar2=None, op0=ALU.mult), inc="vec")
            v_bb = p.step("vector", lambda E, kq=kq: E.tensor_scalar(
                out=bkb[:, kq * 128:(kq + 1) * 128], in0=ones128[:],
                scalar1=bkr[:, kq:kq + 1], scalar2=None, op0=ALU.mult), inc="vec")
        l_bv = dma("gpsimd", "lg_", bvrow[:], bv_d[:, :])

        p.step("vector", lambda E: E.memset(epsc[:], 1e-5), inc="vec")
        p.step("vector", lambda E: E.memset(nm[:], 3.0e38), inc="vec")
        v_init = p.step("vector", lambda E: E.memset(ss[:], 0.0), inc="vec")

        W("tensor", l_bv)
        W("tensor", g_const)
        t_bvb = None
        for h in range(2):
            t_bvb = p.step("tensor", lambda E, h=h: E.matmul(
                pb[6 + h][:, 0:512], ones1[0:1, :], bvrow[0:1, h * 512:(h + 1) * 512],
                start=True, stop=True), inc="ten")
        W("vector", t_bvb)
        v_bvb = None
        for h in range(2):
            v_bvb = p.step("vector", lambda E, h=h: E.tensor_copy(
                out=bvb[:, h * 512:(h + 1) * 512], in_=pb[6 + h][:, 0:512]), inc="vec")

        junk_b = bigB[:, 0:1024]

        x_buf = [bigB[:, 4096 + 2048 * i: 4096 + 2048 * (i + 1)].bitcast(F32) for i in range(2)]
        
        xnT_f = auxf[:, 0:1024]

        def tcol(py, slot):
            return tmps[:, py * 16 + slot: py * 16 + slot + 1]

        x_last_use = [0, 0]
        xn_spill = [0] * C
        q_evac = [0, 0]
        s_cast_c = [0] * C
        qk_evac = {0: 0, 1: 0, 2: 0, 3: 0}
        tr_evac_prev = 0
        lg_evac_prev = 0
        spl_xnt = [0] * C
        stats_pair_tok = [0] * (C // 2)
        sp_tok = [0] * TT
        sp_slot_prev = [0, 0, 0, 0]
        k_mm_last = 0
        exp_par = [0, 0, None]
        v_evac_tok = 0
        spl_v = [0] * C
        v_mm_last = 0
        stats_mm_last = 0
        stats_done = 0
        prev_nmu = 0
        s_cast_last = 0
        last_sexp = 0

        exp_par[2] = v_bvb

        v_xn_tok = [0] * (C + 1)
        lx_tok = [0] * (C + 1)

        def emit_load(cc):
            py = cc % 2
            if cc >= 2:
                W("sync", x_last_use[py])
                W("sync", xn_spill[cc - 2])
            lx_tok[cc] = dma("sync", "ls", x_buf[py][:], x_d[cc * 128:(cc + 1) * 128, :])

        def emit_ln(cc):
            py = cc % 2
            l_x = lx_tok[cc]
            X = x_buf[py]
            W("vector", l_x)
            v_nsum = p.step("vector", lambda E, X=X, py=py: E.reduce_sum(
                out=tcol(py, 0), in_=X[:], axis=AX.X, negate=True), inc="vec")
            W("vector", v_nsum)
            v_nmean = p.step("vector", lambda E, py=py: E.tensor_scalar(
                out=tcol(py, 1), in0=tcol(py, 0), scalar1=1.0 / D, scalar2=None,
                op0=ALU.mult), inc="vec")
            W("scalar", l_x)
            s_ssum = p.step("scalar", lambda E, X=X, py=py: E.activation(
                out=bigB[:, 1024 + py * 1024: 2048 + py * 1024], in_=X[:],
                func=AF.Square, accum_out=tcol(py, 6)), inc="sca")
            W("vector", v_nmean)
            v_m2 = p.step("vector", lambda E, py=py: E.tensor_tensor(
                out=tcol(py, 2), in0=tcol(py, 1), in1=tcol(py, 1), op=ALU.mult), inc="vec")
            W("vector", s_ssum)
            W("vector", v_m2)
            v_var = p.step("vector", lambda E, py=py: E.tensor_scalar(
                out=tcol(py, 3), in0=tcol(py, 6), scalar1=1.0 / D, scalar2=tcol(py, 2),
                op0=ALU.mult, op1=ALU.subtract), inc="vec")
            W("scalar", v_var)
            s_ln = p.step("scalar", lambda E, py=py: E.activation(
                out=tcol(py, 4), in_=tcol(py, 3), func=AF.Ln, bias=epsc[:, 0:1], scale=1.0), inc="sca")
            W("scalar", s_ln)
            s_rstd = p.step("scalar", lambda E, py=py: E.activation(
                out=tcol(py, 4), in_=tcol(py, 4), func=AF.Exp, scale=-0.5), inc="sca")
            W("vector", s_rstd)
            v_xn = p.step("vector", lambda E, X=X, py=py: E.tensor_scalar(
                out=X[:], in0=X[:], scalar1=tcol(py, 1), scalar2=tcol(py, 4),
                op0=ALU.add, op1=ALU.mult), inc="vec")
            v_xn_tok[cc] = v_xn
            W("gpsimd", v_xn)
            xn_spill[cc] = dma("gpsimd", "lg_", XNtok_d[cc * 128:(cc + 1) * 128, :], X[:])

        emit_load(0)
        emit_ln(0)
        emit_load(1)
        emit_ln(1)
        for c in range(C):
            py = c % 2
            X = x_buf[py]
            v_xn = v_xn_tok[c]

            W("tensor", v_xn)
            if c == 0:
                W("tensor", g_const)
            else:
                W("tensor", qk_evac[2])
            t_tr = None
            for k in range(DC):
                t_tr = p.step("tensor", lambda E, X=X, k=k: E.transpose(
                    pb[2 + k // 4][:, (k % 4) * 128:(k % 4) * 128 + 128],
                    X[:, k * 128:(k + 1) * 128], identf[:]), inc="ten")
            x_last_use[py] = t_tr
            if c + 2 < C:
                emit_load(c + 2)
            W("scalar", t_tr)
            v_trev = None
            for h in range(2):
                v_trev = p.step("scalar", lambda E, h=h: E.copy(
                    out=xnT_f[:, h * 512:(h + 1) * 512], in_=pb[2 + h][:, 0:512]), inc="sca")
            tr_evac_prev = v_trev
            W("scalar", v_trev)
            if c >= 2:
                W("scalar", spl_xnt[c - 2])
                W("scalar", k_mm_last)
            s_cast = p.step("scalar", lambda E, py=py: E.copy(
                out=xnTb[:, py * 1024:(py + 1) * 1024], in_=xnT_f[:]), inc="sca")
            s_cast_c[c] = s_cast
            s_cast_last = s_cast

            W("tensor", v_trev)
            if c > 0:
                W("tensor", lg_evac_prev)
            if c == 0:
                W("tensor", l_gw)
                W("tensor", v_bvb)
            t_lgm = None
            for k in range(DC):
                t_lgm = p.step("tensor", lambda E, k=k: E.matmul(
                    pb[6][:, 0:8], xnT_f[:, k * 128:(k + 1) * 128],
                    gwsb[:, k * 8:(k + 1) * 8], start=(k == 0), stop=(k == DC - 1)),
                    inc="ten")
            W("vector", t_lgm)
            v_lg = p.step("vector", lambda E: E.tensor_copy(out=lg[:], in_=pb[6][:, 0:8]),
                          inc="vec")
            lg_evac_prev = v_lg
            W("vector", v_lg)
            v_top = p.step("vector", lambda E, py=py: E.reduce_max(
                out=tcol(py, 7), in_=lg[:], axis=AX.X), inc="vec")
            W("vector", v_top)
            v_eq = p.step("vector", lambda E, py=py: E.tensor_scalar(
                out=lg[:], in0=lg[:], scalar1=tcol(py, 7), scalar2=None,
                op0=ALU.is_equal), inc="vec")
            W("vector", v_eq)
            v_t8 = p.step("vector", lambda E: E.scalar_tensor_tensor(
                out=lg[:], in0=lg[:], scalar=1.0, in1=iotam9[:], op0=ALU.mult,
                op1=ALU.mult), inc="vec")
            W("vector", v_t8)
            v_t9 = p.step("vector", lambda E: E.tensor_scalar(
                out=lg[:], in0=lg[:], scalar1=9.0, scalar2=None, op0=ALU.add), inc="vec")
            W("vector", v_t9)
            v_idx = p.step("vector", lambda E, py=py, c=c: E.tensor_reduce(
                out=sel_all[:, c:c + 1], in_=lg[:], axis=AX.X, op=ALU.min), inc="vec")
            W("scalar", v_top)
            s_e1 = p.step("scalar", lambda E, py=py: E.activation(
                out=tcol(py, 10), in_=tcol(py, 7), func=AF.Exp, scale=-1.0), inc="sca")
            W("scalar", s_e1)
            s_e2 = p.step("scalar", lambda E, py=py: E.add(
                out=tcol(py, 10), in_=tcol(py, 10), add=1.0), inc="sca")
            W("vector", s_e2)
            v_rw = p.step("vector", lambda E, py=py, c=c: E.reciprocal(
                out=rw_all[:, c:c + 1], in_=tcol(py, 10)), inc="vec")
            router_done = v_rw

            xn_bf = xnTb[:, py * 1024:(py + 1) * 1024]
            W("tensor", s_cast)
            W("tensor", v_trev)
            if c == 0:
                W("tensor", g_wk)
            t_k = None
            for qf in range(8):
                for k in range(DC):
                    t_k = p.step("tensor", lambda E, qf=qf, k=k, xn_bf=xn_bf: E.matmul(
                        pb[2 + qf // 4][:, (qf % 4) * 128:(qf % 4) * 128 + 128],
                        wk_sb[:, k * 1024 + qf * 128: k * 1024 + qf * 128 + 128],
                        xn_bf[:, k * 128:(k + 1) * 128],
                        start=(k == 0), stop=(k == DC - 1)), inc="ten")
            k_mm_last = t_k
            W("vector", t_k)
            if c == 0:
                W("vector", v_bb)
            s_ke = None
            for hb in range(2):
                s_ke = p.step("vector", lambda E, hb=hb, c=c: E.tensor_tensor(
                    out=bigA.rearrange("p (k t) -> p k t", k=DC)
                        [:, hb * 4:(hb + 1) * 4, c * 128:(c + 1) * 128],
                    in0=pb[2 + hb][:, 0:512].rearrange("p (q t) -> p q t", q=4),
                    in1=bkb[:, hb * 512:(hb + 1) * 512].rearrange("p (q t) -> p q t", q=4),
                    op=ALU.add), inc="vec")
            qk_evac[2] = qk_evac[3] = s_ke
            if c + 2 < C:
                emit_ln(c + 2)

            if c > 0:
                W("tensor", qk_evac[0])
            if c == 0:
                W("tensor", g_wq)
            t_q = None
            for qf in range(8):
                for k in range(DC):
                    t_q = p.step("tensor", lambda E, qf=qf, k=k, xn_bf=xn_bf: E.matmul(
                        pb[qf // 4][:, (qf % 4) * 128:(qf % 4) * 128 + 128],
                        wq_sb[:, k * 1024 + qf * 128: k * 1024 + qf * 128 + 128],
                        xn_bf[:, k * 128:(k + 1) * 128],
                        start=(k == 0), stop=(k == DC - 1)), inc="ten")
            W("vector", t_q)
            if c == 0:
                W("vector", v_bb)
            if c >= 2:
                W("vector", stats_pair_tok[(c - 2) // 2])
            s_qe = None
            for hb in range(2):
                s_qe = p.step("vector", lambda E, hb=hb, py=py: E.tensor_tensor(
                    out=q_tmp[:, py * 1024 + hb * 512: py * 1024 + hb * 512 + 512],
                    in0=pb[hb][:, 0:512], in1=bqb[:, hb * 512:(hb + 1) * 512],
                    op=ALU.add), inc="vec")
            qk_evac[0] = qk_evac[1] = s_qe
            q_evac[py] = s_qe
            q_mm_last = t_q
            W("gpsimd", s_cast)
            spl_xnt[c] = dma("gpsimd", "lg_",
                XNT_d.rearrange("p (k t) -> p k t", k=DC)[:, :, c * 128:(c + 1) * 128],
                xnTb.rearrange("p (pp k t) -> p pp k t", pp=2, k=DC)[:, py, :, :])

            if py == 1:
                npairs = c + 1
                m_pair = c // 2
                base = m_pair * (m_pair + 1)
                STATS_BANKS = (4, 5, 7)
                W("tensor", q_evac[0])
                W("tensor", q_evac[1])
                W("tensor", qk_evac[2])
                W("tensor", prev_nmu)
                if c == 1:
                    W("vector", v_init)
                for J in range(npairs):
                    pr = J % 3
                    bnk = pb[STATS_BANKS[pr]]
                    if exp_par[pr]:
                        W("tensor", exp_par[pr])
                    t_s = None
                    for half in range(2):
                        for k in range(DC):
                            t_s = p.step("tensor", lambda E, J=J, k=k, half=half, bnk=bnk: E.matmul(
                                bnk[:, half * 128: half * 128 + 128],
                                bigA[:, k * T + J * 128: k * T + J * 128 + 128],
                                q_tmp[:, half * 1024 + k * 128: half * 1024 + k * 128 + 128],
                                start=(k == 0), stop=(k == DC - 1)), inc="ten")
                    stats_mm_last = t_s
                    W("vector", t_s)
                    v_mask = None
                    if J == c - 1:
                        v_mask = p.step("vector", lambda E, bnk=bnk: E.tensor_tensor(
                            out=bnk[:, 0:128], in0=bnk[:, 0:128], in1=cmask[:],
                            op=ALU.add), inc="vec")
                    elif J == c:
                        v_mask = p.step("vector", lambda E, bnk=bnk: E.tensor_scalar(
                            out=bnk[:, 0:128], in0=bnk[:, 0:128], scalar1=-1e30,
                            scalar2=None, op0=ALU.add), inc="vec")
                        W("vector", v_mask)
                        v_mask = p.step("vector", lambda E, bnk=bnk: E.tensor_tensor(
                            out=bnk[:, 128:256], in0=bnk[:, 128:256], in1=cmask[:],
                            op=ALU.add), inc="vec")
                    if v_mask:
                        W("vector", v_mask)
                    v_am = p.step("vector", lambda E, bnk=bnk, pr=pr: E.reduce_max(
                        out=amax[:, pr:pr + 1], in_=bnk[:, 0:256], axis=AX.X,
                        negate=True), inc="vec")
                    W("vector", v_am)
                    v_min = p.step("vector", lambda E, J=J, pr=pr, base=base: E.tensor_scalar(
                        out=cjh[:, base + J:base + J + 1], in0=amax[:, pr:pr + 1],
                        scalar1=nm[:, J:J + 1], scalar2=None, op0=ALU.min), inc="vec")
                    last_vmin = v_min
                    spr = J % 4
                    W("scalar", v_min)
                    W("scalar", sp_slot_prev[spr])
                    s_exp = p.step("scalar", lambda E, J=J, bnk=bnk, spr=spr, base=base: E.activation(
                        out=junk_b[:, spr * 256:(spr + 1) * 256], in_=bnk[:, 0:256],
                        func=AF.Exp, bias=cjh[:, base + J:base + J + 1],
                        accum_out=srow[:, J:J + 1]), inc="sca")
                    exp_par[pr] = s_exp
                    last_sexp = s_exp
                    W("sync", s_exp)
                    sp_tok[base + J] = p.dma("sync",
                        SP_d[:, (base + J) * 256:(base + J) * 256 + 256],
                        junk_b[:, spr * 256:(spr + 1) * 256])
                    sp_slot_prev[spr] = sp_tok[base + J]
                stats_pair_tok[m_pair] = stats_mm_last
                W("vector", last_vmin)
                v_diff = p.step("vector", lambda E, npairs=npairs, base=base: E.tensor_tensor(
                    out=diffb[:, 0:npairs], in0=cjh[:, base:base + npairs],
                    in1=nm[:, 0:npairs], op=ALU.subtract), inc="vec")
                W("vector", v_diff)
                v_nmu = p.step("vector", lambda E, npairs=npairs, base=base: E.tensor_copy(
                    out=nm[:, 0:npairs], in_=cjh[:, base:base + npairs]), inc="vec")
                W("scalar", last_sexp)
                W("scalar", v_diff)
                s_cf = p.step("scalar", lambda E, npairs=npairs: E.activation(
                    out=cfb[:, 0:npairs], in_=diffb[:, 0:npairs], func=AF.Exp), inc="sca")
                sca_p1_last = s_cf
                W("vector", s_cf)
                v_ssu = p.step("vector", lambda E, npairs=npairs: E.tensor_tensor(
                    out=ss[:, 0:npairs], in0=ss[:, 0:npairs], in1=cfb[:, 0:npairs],
                    op=ALU.mult), inc="vec")
                W("vector", v_ssu)
                stats_done = p.step("vector", lambda E, npairs=npairs: E.tensor_tensor(
                    out=ss[:, 0:npairs], in0=ss[:, 0:npairs], in1=srow[:, 0:npairs],
                    op=ALU.add), inc="vec")
                prev_nmu = v_nmu

        W("scalar", stats_done)
        s_lns = p.step("scalar", lambda E: E.activation(
            out=diffb[:, 0:C], in_=ss[:, 0:C], func=AF.Ln, scale=32.0), inc="sca")
        W("vector", s_lns)
        v_nmc = p.step("vector", lambda E: E.tensor_tensor(
            out=nm[:, 0:C], in0=nm[:, 0:C], in1=diffb[:, 0:C], op=ALU.subtract), inc="vec")

        ib = auxf[:, 2048:3072]
        me_t = lambda: ib[:, 0:C]
        scan_t = ib[:, 64:64 + C]
        bo_t = ib[:, 128:128 + C]
        po_t = ib[:, 192:192 + C]
        tmp_t = ib[:, 256:256 + C]
        cnt8 = ib[:, 320:328]
        base8 = ib[:, 328:336]
        slotf = ib[:, 384:384 + C]
        tokid_i = sb("tokid_i", [128, C], mybir.dt.int32)
        slot_i = sb("slot_i", [128, C], mybir.dt.int32)
        pre4 = ib[:, 832:832 + (NSLOT // 128) * 4].bitcast(mybir.dt.int32)
        zrow = sb("zrow", [1, D], BF16)

        g_tok = p.step("gpsimd", lambda E: E.iota(tokid_i[:], pattern=[[128, C]], base=0,
               channel_multiplier=1, allow_small_or_imprecise_dtypes=True), inc="gp")
        W("gpsimd", qk_evac[2])
        g_pre = p.step("gpsimd", lambda E: E.memset(pre4[:], T), inc="gp")
        pre_s = sb("pre_s", [128, NSLOT // 128], mybir.dt.int32)
        g_ps1 = p.step("gpsimd", lambda E: E.iota(pre_s[:], pattern=[[1, NSLOT // 128]],
               base=0, channel_multiplier=NSLOT // 128,
               allow_small_or_imprecise_dtypes=True), inc="gp")
        W("vector", router_done)
        W("vector", g_ps1)
        v_ps = p.step("vector", lambda E: E.tensor_scalar(
            out=pre_s[:], in0=pre_s[:], scalar1=255, scalar2=None,
            op0=ALU.bitwise_and), inc="vec")
        W("vector", v_ps)
        v_ps = p.step("vector", lambda E: E.tensor_scalar(
            out=pre_s[:], in0=pre_s[:], scalar1=T, scalar2=None,
            op0=ALU.add), inc="vec")
        pre4v = pre4.rearrange("p (f j) -> p f j", j=4)
        W("vector", g_pre)
        W("vector", v_ps)
        v_p41 = p.step("vector", lambda E: E.tensor_copy(
            out=pre4v[:, :, 1], in_=pre_s[:]), inc="vec")
        v_p42 = p.step("vector", lambda E: E.tensor_scalar(
            out=pre4v[:, :, 2], in0=pre4v[:, :, 2], scalar1=0, scalar2=None,
            op0=ALU.bitwise_and), inc="vec")
        g_zr = p.step("gpsimd", lambda E: E.memset(zrow[:], 0.0), inc="gp")
        W("gpsimd", g_zr)
        W("gpsimd", v_p41)
        W("gpsimd", v_p42)
        l_pre1 = dma("gpsimd", "lg_", TBL_d.rearrange("(p f) j -> p (f j)", p=128),
                     pre4[:])
        l_zr = dma("gpsimd", "lg_", XNtok_d[T:T + 1, :], zrow[:])

        W("vector", router_done)
        W("vector", qk_evac[2])
        v_bo = p.step("vector", lambda E: E.memset(bo_t[:], 0.0), inc="vec")
        v_po = p.step("vector", lambda E: E.memset(po_t[:], 0.0), inc="vec")
        v_c8 = p.step("vector", lambda E: E.memset(cnt8[:], 0.0), inc="vec")
        scan_done = [None] * 8
        for e in range(8):
            me = me_t()
            W("vector", v_c8)
            v_me = p.step("vector", lambda E, e=e: E.tensor_scalar(
                out=me[:], in0=sel_all[:, 0:C], scalar1=float(e), scalar2=None,
                op0=ALU.is_equal), inc="vec")
            W("vector", v_me)
            v_sc = p.step("vector", lambda E: E.tensor_tensor_scan(
                out=scan_t[:], data0=me[:], data1=me[:], initial=0.0,
                op0=ALU.add, op1=ALU.bypass), inc="vec")
            W("vector", v_sc)
            v_cn = p.step("vector", lambda E, e=e: E.tensor_copy(
                out=cnt8[:, e:e + 1], in_=scan_t[:, C - 1:C]), inc="vec")
            W("vector", v_cn)
            v_p1 = p.step("vector", lambda E: E.tensor_tensor(
                out=tmp_t[:], in0=me[:], in1=scan_t[:], op=ALU.mult), inc="vec")
            W("vector", v_p1)
            v_po = p.step("vector", lambda E: E.tensor_tensor(
                out=po_t[:], in0=po_t[:], in1=tmp_t[:], op=ALU.add), inc="vec")
            scan_done[e] = v_po
        W("tensor", lg_evac_prev)
        W("tensor", scan_done[7])
        t_tc = p.step("tensor", lambda E: E.matmul(
            pb[6][0:8, 0:128], cnt8[:, 0:8], identf[:, 0:128],
            start=True, stop=True), inc="ten")
        W("vector", t_tc)
        v_ct = p.step("vector", lambda E: E.tensor_copy(
            out=ib[:, 448:576][0:8, :], in_=pb[6][0:8, 0:128]), inc="vec")
        W("vector", v_ct)
        v_sc8 = p.step("vector", lambda E: E.tensor_tensor_scan(
            out=ib[:, 576:704][0:8, :], data0=ib[:, 448:576][0:8, :],
            data1=ib[:, 448:576][0:8, :], initial=0.0,
            op0=ALU.add, op1=ALU.bypass), inc="vec")
        W("vector", v_sc8)
        v_ex8 = p.step("vector", lambda E: E.tensor_tensor(
            out=ib[:, 576:704][0:8, :], in0=ib[:, 576:704][0:8, :],
            in1=ib[:, 448:576][0:8, :], op=ALU.subtract), inc="vec")
        W("tensor", v_ex8)
        t_tb = p.step("tensor", lambda E: E.matmul(
            pb[6][:, 128:136], ib[:, 576:704][0:8, :], identf[0:8, 0:8],
            start=True, stop=True), inc="ten")
        W("vector", t_tb)
        v_b8 = p.step("vector", lambda E: E.tensor_copy(
            out=base8[:], in_=pb[6][:, 128:136]), inc="vec")
        lg_evac_prev = v_b8
        for e in range(8):
            me = me_t()
            W("vector", v_b8)
            v_me = p.step("vector", lambda E, e=e: E.tensor_scalar(
                out=me[:], in0=sel_all[:, 0:C], scalar1=float(e), scalar2=None,
                op0=ALU.is_equal), inc="vec")
            W("vector", v_me)
            v_t = p.step("vector", lambda E, e=e: E.tensor_scalar(
                out=tmp_t[:], in0=me[:], scalar1=base8[:, e:e + 1], scalar2=None,
                op0=ALU.mult), inc="vec")
            W("vector", v_t)
            v_bo = p.step("vector", lambda E: E.tensor_tensor(
                out=bo_t[:], in0=bo_t[:], in1=tmp_t[:], op=ALU.add), inc="vec")
        W("vector", v_bo)
        v_s1 = p.step("vector", lambda E: E.tensor_scalar(
            out=slotf[:], in0=sel_all[:, 0:C], scalar1=float(CAP), scalar2=None,
            op0=ALU.mult), inc="vec")
        W("vector", v_s1)
        v_s2 = p.step("vector", lambda E: E.tensor_tensor(
            out=slotf[:], in0=slotf[:], in1=bo_t[:], op=ALU.add), inc="vec")
        W("vector", v_s2)
        v_s3 = p.step("vector", lambda E: E.scalar_tensor_tensor(
            out=slotf[:], in0=slotf[:], scalar=-1.0, in1=po_t[:],
            op0=ALU.add, op1=ALU.add), inc="vec")
        W("vector", v_s3)
        v_s4 = p.step("vector", lambda E: E.tensor_scalar(
            out=slotf[:], in0=slotf[:], scalar1=float(NSLOT - 1), scalar2=None,
            op0=ALU.min), inc="vec")
        W("vector", v_s4)
        v_si = p.step("vector", lambda E: E.tensor_copy(out=slot_i[:], in_=slotf[:]),
                      inc="vec")
        combf = ib[:, 704:832]
        combi = combf.bitcast(mybir.dt.int32)
        combiv = combi.rearrange("p (c j) -> p c j", j=4)
        W("vector", g_tok)
        W("vector", router_done)
        v_cb0 = p.step("vector", lambda E: E.tensor_copy(
            out=combiv[:, :, 0], in_=tokid_i[:]), inc="vec")
        v_cb1 = p.step("vector", lambda E: E.tensor_copy(
            out=combiv[:, :, 1], in_=tokid_i[:]), inc="vec")
        v_cb2 = p.step("vector", lambda E: E.tensor_copy(
            out=combf.rearrange("p (c j) -> p c j", j=4)[:, :, 2],
            in_=rw_all[:, 0:C]), inc="vec")
        W("gpsimd", v_si)
        W("gpsimd", v_cb0)
        W("gpsimd", v_cb1)
        W("gpsimd", v_cb2)
        W("gpsimd", l_pre1)
        from concourse.bass_core_rust import IndirectOffsetOnAxis
        sc_tok = sc_prev = None
        for cc in range(C):
            sc_prev = sc_tok
            sc_tok = p.dma_raw("gpsimd", lambda E, cc=cc: E.indirect_dma_start(
                out=TBL_d[:, :], out_offset=IndirectOffsetOnAxis(slot_i[:, cc:cc + 1], 0),
                in_=combi[:, cc * 4:cc * 4 + 4], in_offset=None))
        scatters_done = (sc_prev, sc_tok)

        xv_use = [0, 0]
        vt_done = [0] * C
        for t in range(C):
            py = t % 2
            W("sync", spl_xnt[t])
            if t < 2:
                W("sync", k_mm_last)
                W("sync", spl_xnt[C - 2])
                W("sync", spl_xnt[C - 1])
            else:
                W("sync", xv_use[py])
            l_xv = dma("sync", "ls",
                xnTb.rearrange("p (pp k t) -> p pp k t", pp=2, k=DC)[:, py, :, :],
                XNT_d.rearrange("p (k t) -> p k t", k=DC)[:, :, t * 128:(t + 1) * 128])
            W("tensor", l_xv)
            if t == 0:
                W("tensor", g_wv)
                W("tensor", q_evac[0])
                W("tensor", q_evac[1])
            if t == 1:
                W("tensor", qk_evac[2])
            if t >= 2:
                W("tensor", vt_done[t - 2])
            bb = 2 * (t % 2)
            t_v = None
            for h in range(2):
                for k in range(DC):
                    t_v = p.step("tensor", lambda E, h=h, k=k, py=py, bb=bb: E.matmul(
                        pb[bb + h][:, 0:512],
                        xnTb[:, py * 1024 + k * 128: py * 1024 + k * 128 + 128],
                        wv_sb[:, k * 1024 + h * 512: k * 1024 + h * 512 + 512],
                        start=(k == 0), stop=(k == DC - 1)), inc="ten")
            xv_use[py] = t_v
            W("vector", t_v)
            if t == 0:
                W("vector", v_bvb)
                W("vector", sca_p1_last)
                W("vector", last_sexp)
                for _pr in range(4):
                    W("vector", sp_slot_prev[_pr])
                W("vector", xn_spill[C - 1])
                W("vector", xn_spill[C - 2])
            v_vt = None
            for h in range(2):
                v_vt = p.step("vector", lambda E, h=h, t=t, bb=bb: E.tensor_tensor(
                    out=bigB[:, t * 1024 + h * 512: t * 1024 + h * 512 + 512],
                    in0=pb[bb + h][:, 0:512], in1=bvb[:, h * 512:(h + 1) * 512],
                    op=ALU.add), inc="vec")
            vt_done[t] = v_vt
        xv_last = xv_use[(C - 1) % 2]

        pt_buf = [q_tmp[:, i * 256:(i + 1) * 256] for i in range(8)]
        rdev = [wsb[:, 1024 + 1024 * i: 1024 + 1024 * (i + 1)] for i in range(2)]

        def expert_w(par):
            base = 12288 * (1 - par)
            return (wsb[:, base: base + 4096], wsb[:, base + 4096: base + 8192],
                    wsb[:, base + 8192: base + 12288])

        W("gpsimd", xv_last)
        wg0, wu0, wd0 = expert_w(0)
        g_we = [None] * E
        wg_t = dma("gpsimd", "lg_", wg0.rearrange("p (k f) -> p k f", k=DC),
                   wg_d[0].rearrange("(k p) f -> p k f", p=128))
        wu_t = dma("gpsimd", "lg_", wu0.rearrange("p (k f) -> p k f", k=DC),
                   wu_d[0].rearrange("(k p) f -> p k f", p=128))
        wd_t = dma("gpsimd", "lg_", wd0.rearrange("p (q f) -> p q f", q=IC),
                   wd_d[0].rearrange("(q p) f -> p q f", p=128))
        g_we[0] = (wg_t, wu_t, wd_t)

        pt_used = [0] * 8
        acc_evac = [0, 0]
        rd_spill = [0] * C
        rdev_free = [0, 0]
        last_pv = 0
        s_cf = 0
        v_pt = 0
        pair_i = 0
        for g in range(C // 2):
            t0, t1 = 2 * g, 2 * g + 1
            n = t1 + 1
            base = g * (g + 1)
            W("vector", v_nmc)
            W("vector", s_cf)
            v_df2 = p.step("vector", lambda E, n=n, base=base: E.tensor_tensor(
                out=diffb[:, 0:n], in0=nm[:, 0:n], in1=cjh[:, base:base + n],
                op=ALU.subtract), inc="vec")
            W("scalar", v_df2)
            W("scalar", v_pt)
            s_cf = p.step("scalar", lambda E, n=n: E.activation(
                out=cfP[:, 0:n], in_=diffb[:, 0:n], func=AF.Exp), inc="sca")
            for J in range(n):
                pr = (pair_i + J) % 8
                W("sync", pt_used[pr])
                if base + J < 8:
                    W("sync", stats_mm_last)
                W("sync", sp_tok[base + J])
                l_t = p.dma("sync", pt_buf[pr][:, 0:256],
                            SP_d[:, (base + J) * 256:(base + J) * 256 + 256])
                W("vector", l_t)
                W("vector", s_cf)
                v_pt = p.step("vector", lambda E, J=J, pr=pr: E.tensor_scalar(
                    out=pt_buf[pr][:, 0:256], in0=pt_buf[pr][:, 0:256],
                    scalar1=cfP[:, J:J + 1], scalar2=None, op0=ALU.mult), inc="vec")
                W("tensor", v_pt)
                if J == 0:
                    W("tensor", vt_done[t1])
                    W("tensor", acc_evac[0])
                    W("tensor", acc_evac[1])
                for t in (t0, t1):
                    if J > t:
                        continue
                    pcol = (t - t0) * 128
                    for h in range(2):
                        last_pv = p.step("tensor", lambda E, t=t, J=J, h=h, pcol=pcol, pr=pr: E.matmul(
                            pb[(t % 2) * 2 + h][:, 0:512],
                            pt_buf[pr][:, pcol: pcol + 128],
                            bigB[:, J * 1024 + h * 512: J * 1024 + h * 512 + 512],
                            start=(J == 0), stop=(J == t)), inc="ten")
                    pt_used[pr] = last_pv
            pair_i += n
            for t in (t0, t1):
                W("vector", last_pv)
                W("vector", rdev_free[t % 2])
                v_rd = None
                for h in range(2):
                    v_rd = p.step("vector", lambda E, t=t, h=h: E.tensor_copy(
                        out=rdev[t % 2][:, h * 512:(h + 1) * 512],
                        in_=pb[(t % 2) * 2 + h][:, 0:512]), inc="vec")
                acc_evac[t % 2] = v_rd
                W("scalar", v_rd)
                rd_spill[t] = dma("scalar", "lo", RD_d[t * 128:(t + 1) * 128, :],
                                  rdev[t % 2][:])
                rdev_free[t % 2] = rd_spill[t]

        SPE = CAP // 128
        assert SPE >= 4
        S = 8 * SPE
        xg_of = lambda u: bigB[:, (u % 8) * 1024: (u % 8) * 1024 + 1024]
        xgT_of = lambda u: bigB[:, 8192 + (u % 2) * 1024: 8192 + (u % 2) * 1024 + 1024]
        out_of = lambda u: bigB[:, 10240 + (u % 2) * 1024: 10240 + (u % 2) * 1024 + 1024]
        tbl_sb = sb("tbl_sb", [128, 64], mybir.dt.int32)
        tbl_fb = tbl_sb.bitcast(F32)

        def G_of(u):
            return auxf[:, (u % 2) * 1536: (u % 2) * 1536 + 512]

        def A_of(u):
            return auxf[:, (u % 2) * 1536 + 512: (u % 2) * 1536 + 1024]

        def SIL_of(u):
            return auxf[:, (u % 2) * 1536 + 1024: (u % 2) * 1536 + 1536]

        def h_of(u):
            return xnTb[:, (u % 2) * 512: (u % 2) * 512 + 512]

        def ht_of(u):
            return xnTb[:, 1024 + (u % 2) * 512: 1024 + (u % 2) * 512 + 512]

        tb_tok = [0] * S
        xsc = [bigB[:, 12288 + 2048 * i: 12288 + 2048 * (i + 1)].bitcast(F32) for i in range(2)]
        rsc = [bigB[:, 16384 + 1024 * i: 16384 + 1024 * (i + 1)] for i in range(2)]
        p4a_done = [0] * C
        p4a_free = [0, 0]
        p4a_t = 0
        gx_tok = [0] * S
        xgT_ev = [0] * S
        t_xgT = [0] * S
        t_gu_tok = [0] * S
        v_h_tok = [0] * S
        ht_ev = [0] * S
        ht_cons = [0] * S
        out_ev = [0] * S
        out_sc = [0] * S
        gu_free = [acc_evac[0], acc_evac[1]]
        ta_free = [last_sexp, last_sexp]
        htp_free = lg_evac_prev
        down_free = exp_par[2]
        h_free = [0, 0]
        aux_free = [0, 0]
        exp_done = [0] * E
        last_acc_add = 0
        last_sc = 0

        for u in range(S + 5):
            vs = [0, 1, 2, 3, 4, 5] if u == 0 else [u + 5]
            for v in vs:
                if v >= S:
                    continue
                if v >= 16:
                    W("sync", out_sc[v - 16])
                if v < 4:
                    for sct in scatters_done:
                        W("sync", sct)
                tb_tok[v] = p.dma("sync", tbl_sb[:, (v % 16) * 4: (v % 16) * 4 + 4],
                                  TBL_d[v * 128:(v + 1) * 128, :])
            gvs = [0, 1, 2, 3, 4] if u == 0 else [u + 4]
            for v in gvs:
                if v >= S:
                    continue
                W("gpsimd", tb_tok[v])
                if v < 5:
                    W("gpsimd", last_pv)
                    W("gpsimd", l_zr)
                else:
                    W("gpsimd", t_xgT[v - 5])
                gx_tok[v] = p.dma_raw("gpsimd", lambda E, v=v: E.indirect_dma_start(
                    out=xg_of(v)[:, :], out_offset=None, in_=XNtok_d[:, :],
                    in_offset=IndirectOffsetOnAxis(
                        tbl_sb[:, (v % 16) * 4: (v % 16) * 4 + 1], 0)))

            ub = u - 3
            if 0 <= ub < S:
                W("tensor", v_h_tok[ub])
                W("tensor", htp_free)
                htp = pb[6][:, 0:512].bitcast(BF16)
                t_ht = None
                for qi in range(IC):
                    t_ht = p.step("tensor", lambda E_, qi=qi, htp=htp, ub=ub: E_.transpose(
                        htp[:, qi * 128:(qi + 1) * 128],
                        h_of(ub)[:, qi * 128:(qi + 1) * 128], identb[:]), inc="ten")
                h_free[ub % 2] = t_ht
                W("scalar", t_ht)
                if ub >= 2:
                    W("scalar", ht_cons[ub - 2])
                s_ht = p.step("scalar", lambda E_, htp=htp, ub=ub: E_.copy(
                    out=ht_of(ub), in_=htp[:, 0:512]), inc="sca")
                ht_ev[ub] = s_ht
                htp_free = s_ht

            uc = u - 4
            if 0 <= uc < S:
                ec = uc // SPE
                wde = expert_w(ec % 2)[2]
                W("tensor", ht_ev[uc])
                t_d = None
                for hh in range(2):
                    W("tensor", down_free)
                    for qi in range(IC):
                        t_d = p.step("tensor", lambda E_, qi=qi, hh=hh, wde=wde, uc=uc: E_.matmul(
                            pb[7][:, 0:512],
                            ht_of(uc)[:, qi * 128:(qi + 1) * 128],
                            wde[:, qi * 1024 + hh * 512: qi * 1024 + hh * 512 + 512],
                            start=(qi == 0), stop=(qi == IC - 1)), inc="ten")
                    W("scalar", t_d)
                    if uc >= 2:
                        W("scalar", out_sc[uc - 2])
                    last_acc_add = p.step("scalar", lambda E_, uc=uc, hh=hh: E_.copy(
                        out=out_of(uc)[:, hh * 512:(hh + 1) * 512], in_=pb[7][:, 0:512]),
                        inc="sca")
                    down_free = last_acc_add
                ht_cons[uc] = t_d
                if uc == (ec + 1) * SPE - 1:
                    exp_done[ec] = t_d
                out_ev[uc] = last_acc_add
                W("gpsimd", last_acc_add)
                W("gpsimd", tb_tok[uc])
                out_sc[uc] = p.dma_raw("gpsimd", lambda E, uc=uc: E.indirect_dma_start(
                    out=FFNS_d[:, :], out_offset=IndirectOffsetOnAxis(
                        tbl_sb[:, (uc % 16) * 4 + 1: (uc % 16) * 4 + 2], 0),
                    in_=out_of(uc)[:, :], in_offset=None))
                last_sc = out_sc[uc]

            if u < S:
                e, t = divmod(u, SPE)
                wge, wue, wde_ = expert_w(e % 2)
                if e + 1 < E and t == 3:
                    wgn, wun, wdn = expert_w((e + 1) % 2)
                    if e == 0:
                        W("gpsimd", last_pv)
                        W("gpsimd", acc_evac[0])
                        W("gpsimd", acc_evac[1])
                        W("gpsimd", rdev_free[0])
                        W("gpsimd", rdev_free[1])
                    else:
                        W("gpsimd", exp_done[e - 1])
                    wg_t = dma("gpsimd", "lg_", wgn.rearrange("p (k f) -> p k f", k=DC),
                               wg_d[e + 1].rearrange("(k p) f -> p k f", p=128))
                    wu_t = dma("gpsimd", "lg_", wun.rearrange("p (k f) -> p k f", k=DC),
                               wu_d[e + 1].rearrange("(k p) f -> p k f", p=128))
                    wd_t = dma("gpsimd", "lg_", wdn.rearrange("p (q f) -> p q f", q=IC),
                               wd_d[e + 1].rearrange("(q p) f -> p q f", p=128))
                    g_we[e + 1] = (wg_t, wu_t, wd_t)
                W("tensor", gx_tok[u])
                W("tensor", ta_free[u % 2])
                tap = pb[4 + u % 2][:, 0:512].bitcast(BF16)
                t_tx = None
                for k in range(DC):
                    t_tx = p.step("tensor", lambda E_, k=k, tap=tap, u=u: E_.transpose(
                        tap[:, k * 128:(k + 1) * 128],
                        xg_of(u)[:, k * 128:(k + 1) * 128], identb[:]), inc="ten")
                t_xgT[u] = t_tx
                W("scalar", t_tx)
                if u >= 2:
                    W("scalar", t_gu_tok[u - 2])
                s_xe = p.step("scalar", lambda E_, u=u: E_.copy(
                    out=xgT_of(u), in_=pb[4 + u % 2][:, 0:512].bitcast(BF16)[:, 0:1024]),
                    inc="sca")
                xgT_ev[u] = s_xe
                ta_free[u % 2] = s_xe

            if p4a_t < C and u >= 2 and (u % 3 != 2 or S + 6 - u <= C - p4a_t + 2):
                tt = p4a_t
                q4 = tt % 2
                if tt < 2:
                    W("sync", last_pv)
                    W("sync", stats_mm_last)
                else:
                    W("sync", p4a_done[tt - 2])
                W("sync", rd_spill[tt])
                W("sync", xn_spill[C - 1])
                W("sync", xn_spill[C - 2])
                l1 = p.dma("sync", xsc[q4][:, :], x_d[tt * 128:(tt + 1) * 128, :])
                l2 = p.dma("sync", bigA[:, tt * 1024:(tt + 1) * 1024],
                           XNtok_d[tt * 128:(tt + 1) * 128, :])
                l3 = p.dma("sync", rsc[q4][:, :], RD_d[tt * 128:(tt + 1) * 128, :])
                W("vector", l1)
                W("vector", l2)
                v_a1 = p.step("vector", lambda E_, tt=tt, q4=q4: E_.tensor_tensor(
                    out=bigA[:, tt * 1024:(tt + 1) * 1024],
                    in0=bigA[:, tt * 1024:(tt + 1) * 1024], in1=xsc[q4][:, :],
                    op=ALU.add), inc="vec")
                W("vector", l3)
                W("vector", v_a1)
                v_a2 = p.step("vector", lambda E_, tt=tt, q4=q4: E_.tensor_tensor(
                    out=bigA[:, tt * 1024:(tt + 1) * 1024],
                    in0=bigA[:, tt * 1024:(tt + 1) * 1024], in1=rsc[q4][:, :],
                    op=ALU.add), inc="vec")
                p4a_done[tt] = v_a2
                p4a_free[q4] = v_a2
                p4a_t += 1

            ua = u - 1
            if 0 <= ua < S:
                e, t = divmod(ua, SPE)
                wge, wue, _ = expert_w(e % 2)
                sp = ua % 2
                if ua == 0:
                    for wt in g_we[0]:
                        W("tensor", wt)
                elif ua % SPE == 0:
                    for wt in g_we[ua // SPE]:
                        W("tensor", wt)
                W("tensor", xgT_ev[ua])
                W("tensor", gu_free[sp])
                t_gu = None
                for k in range(DC):
                    p.step("tensor", lambda E_, k=k, sp=sp, wge=wge, ua=ua: E_.matmul(
                        pb[sp * 2][:, 0:512],
                        xgT_of(ua)[:, k * 128:(k + 1) * 128],
                        wge[:, k * 512:(k + 1) * 512],
                        start=(k == 0), stop=(k == DC - 1)), inc="ten")
                    t_gu = p.step("tensor", lambda E_, k=k, sp=sp, wue=wue, ua=ua: E_.matmul(
                        pb[sp * 2 + 1][:, 0:512],
                        xgT_of(ua)[:, k * 128:(k + 1) * 128],
                        wue[:, k * 512:(k + 1) * 512],
                        start=(k == 0), stop=(k == DC - 1)), inc="ten")
                t_gu_tok[ua] = t_gu
                sE = tbl_fb[:, (ua % 16) * 4 + 2: (ua % 16) * 4 + 3]
                W("vector", t_gu)
                W("vector", tb_tok[ua])
                W("vector", aux_free[sp])
                v_G = p.step("vector", lambda E_, sE=sE, sp=sp, ua=ua: E_.tensor_scalar(
                    out=G_of(ua), in0=pb[sp * 2][:, 0:512], scalar1=sE, scalar2=None,
                    op0=ALU.mult), inc="vec")
                W("scalar", v_G)
                s_sil = p.step("scalar", lambda E_, ua=ua: E_.activation(
                    out=SIL_of(ua), in_=G_of(ua), func=AF.Sigmoid), inc="sca")
                W("vector", v_G)
                v_A = p.step("vector", lambda E_, sE=sE, sp=sp, ua=ua: E_.scalar_tensor_tensor(
                    out=A_of(ua), in0=pb[sp * 2 + 1][:, 0:512], scalar=sE, in1=G_of(ua),
                    op0=ALU.mult, op1=ALU.mult), inc="vec")
                gu_free[sp] = v_A
                W("vector", s_sil)
                W("vector", v_A)
                W("vector", h_free[sp])
                v_h = p.step("vector", lambda E_, ua=ua: E_.tensor_tensor(
                    out=h_of(ua), in0=A_of(ua), in1=SIL_of(ua), op=ALU.mult), inc="vec")
                v_h_tok[ua] = v_h
                aux_free[sp] = v_h
        down_evac_prev = last_acc_add

        fff = [wsb[:, 1024 * i: 1024 * (i + 1)] for i in range(2)]
        fo = [wsb[:, 20480 + 2048 * i: 20480 + 2048 * (i + 1)].bitcast(F32) for i in range(2)]
        o_reads = [0, 0]
        o_store = [0, 0]
        st_last = 0
        for t in range(C):
            tp = t % 2
            if t == 0:
                W("sync", exp_done[6])
                W("sync", exp_done[E - 1])
                W("sync", out_sc[S - 1])
                W("sync", out_sc[S - 2])
            if t >= 2:
                W("sync", o_reads[tp])
            l4 = dma("sync", "ls", fff[tp][:], FFNS_d[t * 128:(t + 1) * 128, :])
            W("vector", l4)
            W("vector", p4a_done[t])
            W("vector", o_store[tp])
            v_o3 = p.step("vector", lambda E_, tp=tp, t=t: E_.tensor_tensor(
                out=fo[tp][:], in0=bigA[:, t * 1024:(t + 1) * 1024], in1=fff[tp][:],
                op=ALU.add), inc="vec")
            o_reads[tp] = v_o3
            W("scalar", v_o3)
            st_last = dma("scalar", "lo", out_d[t * 128:(t + 1) * 128, :], fo[tp][:])
            o_store[tp] = st_last
        W("scalar", st_last)

        with nc.Block() as block:
            for eng in ENGS:
                getattr(block, eng)(lambda E, eng=eng: [f(E) for f in p.q[eng]])
    return nc


def prep_inputs(x_full, Wq, bq, Wk, bk, Wv, bv, gate_w, w_gate, w_up, w_down):
    B = x_full.shape[0]
    c = np.ascontiguousarray
    shared = {
        "Wq": c(Wq), "Wk": c(Wk), "Wv": c(Wv),
        "bq2": c(bq.reshape(8, 128).T), "bk2": c(bk.reshape(8, 128).T),
        "bvrow": c(bv.reshape(1, 1024)),
        "gwT": c(gate_w.T),
        "wgT": c(np.transpose(w_gate, (0, 2, 1))),
        "wuT": c(np.transpose(w_up, (0, 2, 1))),
        "wdT": c(np.transpose(w_down, (0, 2, 1))),
    }
    return [dict(shared, x=c(x_full[b])) for b in range(B)]


def kernel(**inputs):
    x = np.asarray(inputs["L0_attn_minibatch"])
    in_maps = prep_inputs(
        x, np.asarray(inputs["L0_attn_Wq"]), np.asarray(inputs["L0_attn_bq"]),
        np.asarray(inputs["L0_attn_Wk"]), np.asarray(inputs["L0_attn_bk"]),
        np.asarray(inputs["L0_attn_Wv"]), np.asarray(inputs["L0_attn_bv"]),
        np.asarray(inputs["L0_ffn_gate_w"]), np.asarray(inputs["L0_ffn_w_gate"]),
        np.asarray(inputs["L0_ffn_w_up"]), np.asarray(inputs["L0_ffn_w_down"]))
    nc = build(T=x.shape[1])
    res = run_bass_kernel_spmd(nc, in_maps, core_ids=list(range(x.shape[0])))
    return np.stack([r["out"] for r in res.results], axis=0)


# version 36
# speedup vs baseline: 1.0466x; 1.0013x over previous
import sys
sys.path.insert(0, "/opt/trn_rl_repo")
from contextlib import ExitStack
import numpy as np
import concourse.bass as bass
import concourse.mybir as mybir
from concourse.bass_utils import run_bass_kernel_spmd

F32 = mybir.dt.float32
BF16 = mybir.dt.bfloat16
AF = mybir.ActivationFunctionType
ALU = mybir.AluOpType
AX = mybir.AxisListType

ENGS = ("sync", "gpsimd", "vector", "scalar", "tensor")


class Plan:

    DMA_SEMS = {"sync": ("lsa", "lsb"), "gpsimd": ("lga", "lgb"),
                "scalar": ("loa", "lob")}

    def __init__(self, nc, es):
        self.nc = nc
        self.es = es
        self.q = {e: [] for e in ENGS}
        self.h = {}
        self.cnt = {}
        self.dma_par = {e: 0 for e in self.DMA_SEMS}
        for ss in [x for pair in self.DMA_SEMS.values() for x in pair]:
            self.sem(ss)

    def sem(self, name):
        self.h[name] = self.es.enter_context(self.nc.semaphore(name))
        self.cnt[name] = 0

    def wait(self, eng, tok):
        if not tok:
            return
        sname, val = tok
        if not val:
            return
        assert 0 < val <= self.cnt[sname], (sname, val, self.cnt[sname])
        h = self.h[sname]
        self.q[eng].append(lambda E, h=h, val=val: E.wait_ge(h, val))

    def step(self, eng, fn, inc=None, k=1):
        if inc is None:
            self.q[eng].append(fn)
            return None
        h = self.h[inc]
        self.cnt[inc] += k

        def run(E, fn=fn, h=h, k=k):
            fn(E).then_inc(h, k)

        self.q[eng].append(run)
        return (inc, self.cnt[inc])

    def dma(self, eng, out, in_):
        return self.dma_raw(eng, lambda E, o=out, i=in_: E.dma_start(out=o, in_=i))

    def dma_raw(self, eng, fn):
        sname = self.DMA_SEMS[eng][self.dma_par[eng]]
        self.dma_par[eng] ^= 1
        self.wait(eng, (sname, self.cnt[sname]))
        return self.step(eng, fn, inc=sname, k=16)


def build(T=4096):
    C = T // 128
    assert C % 2 == 0 and C >= 4
    D, DC, E, I, IC = 1024, 8, 8, 512, 4

    nc = bass.Bass("TRN2", target_bir_lowering=False)

    x_d = nc.dram_tensor("x", [T, D], F32, kind="ExternalInput")
    wq_d = nc.dram_tensor("Wq", [D, D], F32, kind="ExternalInput")
    wk_d = nc.dram_tensor("Wk", [D, D], F32, kind="ExternalInput")
    wv_d = nc.dram_tensor("Wv", [D, D], F32, kind="ExternalInput")
    bq_d = nc.dram_tensor("bq2", [128, 8], F32, kind="ExternalInput")
    bk_d = nc.dram_tensor("bk2", [128, 8], F32, kind="ExternalInput")
    bv_d = nc.dram_tensor("bvrow", [1, D], F32, kind="ExternalInput")
    gw_d = nc.dram_tensor("gwT", [D, E], F32, kind="ExternalInput")
    wg_d = nc.dram_tensor("wgT", [E, D, I], F32, kind="ExternalInput")
    wu_d = nc.dram_tensor("wuT", [E, D, I], F32, kind="ExternalInput")
    wd_d = nc.dram_tensor("wdT", [E, I, D], F32, kind="ExternalInput")
    out_d = nc.dram_tensor("out", [T, D], F32, kind="ExternalOutput")
    TT = ((T // 128) // 2) * ((T // 128) // 2 + 1)
    SP_d = nc.dram_tensor("SPs", [128, TT * 256], BF16, kind="Internal")
    XNT_d = nc.dram_tensor("XNTs", [128, DC * T], BF16, kind="Internal")
    XNtok_d = nc.dram_tensor("XNtoks", [T + 1, D], BF16, kind="Internal")
    CAP = (5 * T) // 32 if (5 * T) // 32 >= 512 else T // 4
    NSLOT = 8 * CAP
    TBL_d = nc.dram_tensor("TBLs", [NSLOT, 4], mybir.dt.int32, kind="Internal")
    FFNS_d = nc.dram_tensor("FFNSs", [T + 256, D], BF16, kind="Internal")
    RD_d = nc.dram_tensor("RDs", [T, D], BF16, kind="Internal")

    es = ExitStack()
    with es:
        sb = lambda name, shape, dt: es.enter_context(nc.sbuf_tensor(name, shape, dt))
        bigA = sb("bigA", [128, DC * T], BF16)
        bigB = sb("bigB", [128, max(C * 1024, 18432)], BF16)
        wsb = sb("wsb", [128, 24576], BF16)
        q_tmp = sb("q_tmp", [128, 2048], BF16)
        xnTb = sb("xnTb", [128, 2048], BF16)
        auxf = sb("auxf", [128, 3072], F32)
        cmask = sb("cmask", [128, 128], F32)
        identf = sb("identf", [128, 128], F32)
        identb = sb("identb", [128, 128], BF16)
        iota8 = sb("iota8", [128, 8], F32)
        iotam9 = sb("iotam9", [128, 8], F32)
        ones1 = sb("ones1", [1, 128], BF16)
        bvrow = sb("bvrow_s", [1, D], BF16)
        bvb = sb("bvb", [128, D], BF16)
        onesr = sb("onesr_s", [1, 256], BF16)
        gwsb = sb("gwsb", [128, DC * E], F32)
        nm = sb("nm", [128, C], F32)
        ss = sb("ss", [128, C], F32)
        cjh = sb("cjh", [128, (C // 2) * (C // 2 + 1)], F32)
        cfP = sb("cfP", [128, C], F32)
        sel_all = sb("sel_all", [128, C], F32)
        rw_all = sb("rw_all", [128, C], F32)
        lg = sb("lg", [128, 8], F32)
        amax = sb("amax", [128, 4], F32)
        diffb = sb("diffb", [128, C], F32)
        cfb = sb("cfb", [128, C], F32)
        srow = sb("srow", [128, C], F32)
        tmps = sb("tmps", [128, 32], F32)
        epsc = sb("epsc", [128, 1], F32)
        pb = [es.enter_context(nc.psum_tensor(f"pb{i}", [128, 512], F32)) for i in range(8)]

        p = Plan(nc, es)
        for sname in ("ten", "vec", "sca", "gp"):
            p.sem(sname)
        W = p.wait
        dma = lambda eng, _sem, out, in_: p.dma(eng, out, in_)

        p.step("gpsimd", lambda E: E.iota(iota8[:], pattern=[[1, 8]], base=0,
               channel_multiplier=0, allow_small_or_imprecise_dtypes=True), inc="gp")
        p.step("gpsimd", lambda E: E.iota(iotam9[:], pattern=[[1, 8]], base=-9,
               channel_multiplier=0, allow_small_or_imprecise_dtypes=True), inc="gp")
        g0 = p.step("gpsimd", lambda E: E.memset(cmask[:], 0.0), inc="gp")
        W("gpsimd", g0)
        g0 = p.step("gpsimd", lambda E: E.affine_select(out=cmask[:], in_=cmask[:],
               pattern=[[1, 128]], compare_op=ALU.is_ge, fill=-1e30, base=0,
               channel_multiplier=-1), inc="gp")
        g0 = p.step("gpsimd", lambda E: E.memset(identf[:], 1.0), inc="gp")
        W("gpsimd", g0)
        g0 = p.step("gpsimd", lambda E: E.affine_select(out=identf[:], in_=identf[:],
               pattern=[[1, 128]], compare_op=ALU.is_equal, fill=0.0, base=0,
               channel_multiplier=-1), inc="gp")
        g0 = p.step("gpsimd", lambda E: E.memset(identb[:], 1.0), inc="gp")
        W("gpsimd", g0)
        g0 = p.step("gpsimd", lambda E: E.affine_select(out=identb[:], in_=identb[:],
               pattern=[[1, 128]], compare_op=ALU.is_equal, fill=0.0, base=0,
               channel_multiplier=-1), inc="gp")
        g_const = p.step("gpsimd", lambda E: E.memset(ones1[:], 1.0), inc="gp")

        wq_sb, wk_sb, wv_sb = wsb[:, 0:8192], wsb[:, 8192:16384], wsb[:, 16384:24576]
        g_wq = dma("gpsimd", "lg_", wq_sb.rearrange("p (k f) -> p k f", k=DC),
                   wq_d.rearrange("(k p) f -> p k f", p=128))
        g_wk = dma("gpsimd", "lg_", wk_sb.rearrange("p (k f) -> p k f", k=DC),
                   wk_d.rearrange("(k p) f -> p k f", p=128))
        g_wv = dma("gpsimd", "lg_", wv_sb.rearrange("p (k f) -> p k f", k=DC),
                   wv_d.rearrange("(k p) f -> p k f", p=128))

        l_gw = dma("sync", "ls", gwsb[:].rearrange("p (k e) -> p k e", k=DC),
                   gw_d.rearrange("(k p) e -> p k e", p=128))
        bqb = auxf[:, 1024:2048]
        bkb = auxf[:, 2048:3072]
        bqr = sb("bqr_s", [128, 8], F32)
        bkr = sb("bkr_s", [128, 8], F32)
        l_bq = dma("sync", "ls", bqr[:], bq_d[:, :])
        l_bk = dma("sync", "ls", bkr[:], bk_d[:, :])
        ones128 = bigB[:, 3072:3584].bitcast(F32)[:, 0:128]
        v_ones = p.step("vector", lambda E: E.memset(ones128[:], 1.0), inc="vec")
        W("vector", l_bq)
        W("vector", l_bk)
        W("vector", v_ones)
        v_bb = None
        for kq in range(8):
            p.step("vector", lambda E, kq=kq: E.tensor_scalar(
                out=bqb[:, kq * 128:(kq + 1) * 128], in0=ones128[:],
                scalar1=bqr[:, kq:kq + 1], scalar2=None, op0=ALU.mult), inc="vec")
            v_bb = p.step("vector", lambda E, kq=kq: E.tensor_scalar(
                out=bkb[:, kq * 128:(kq + 1) * 128], in0=ones128[:],
                scalar1=bkr[:, kq:kq + 1], scalar2=None, op0=ALU.mult), inc="vec")
        l_bv = dma("gpsimd", "lg_", bvrow[:], bv_d[:, :])

        p.step("vector", lambda E: E.memset(epsc[:], 1e-5), inc="vec")
        p.step("vector", lambda E: E.memset(nm[:], 3.0e38), inc="vec")
        v_init = p.step("vector", lambda E: E.memset(ss[:], 0.0), inc="vec")

        W("tensor", l_bv)
        W("tensor", g_const)
        t_bvb = None
        for h in range(2):
            t_bvb = p.step("tensor", lambda E, h=h: E.matmul(
                pb[6 + h][:, 0:512], ones1[0:1, :], bvrow[0:1, h * 512:(h + 1) * 512],
                start=True, stop=True), inc="ten")
        W("vector", t_bvb)
        v_bvb = None
        for h in range(2):
            v_bvb = p.step("vector", lambda E, h=h: E.tensor_copy(
                out=bvb[:, h * 512:(h + 1) * 512], in_=pb[6 + h][:, 0:512]), inc="vec")

        junk_b = bigB[:, 0:1024]

        x_buf = [bigB[:, 4096 + 2048 * i: 4096 + 2048 * (i + 1)].bitcast(F32) for i in range(2)]
        
        xnT_f = auxf[:, 0:1024]

        def tcol(py, slot):
            return tmps[:, py * 16 + slot: py * 16 + slot + 1]

        x_last_use = [0, 0]
        xn_spill = [0] * C
        q_evac = [0, 0]
        s_cast_c = [0] * C
        qk_evac = {0: 0, 1: 0, 2: 0, 3: 0}
        tr_evac_prev = 0
        lg_evac_prev = 0
        spl_xnt = [0] * C
        stats_pair_tok = [0] * (C // 2)
        sp_tok = [0] * TT
        sp_slot_prev = [0, 0, 0, 0]
        k_mm_last = 0
        exp_par = [0, 0, None]
        v_evac_tok = 0
        spl_v = [0] * C
        v_mm_last = 0
        stats_mm_last = 0
        stats_done = 0
        prev_nmu = 0
        s_cast_last = 0
        last_sexp = 0

        exp_par[2] = v_bvb

        v_xn_tok = [0] * (C + 1)
        lx_tok = [0] * (C + 1)

        def emit_load(cc):
            py = cc % 2
            if cc >= 2:
                W("sync", x_last_use[py])
                W("sync", xn_spill[cc - 2])
            lx_tok[cc] = dma("sync", "ls", x_buf[py][:], x_d[cc * 128:(cc + 1) * 128, :])

        def emit_ln(cc):
            py = cc % 2
            l_x = lx_tok[cc]
            X = x_buf[py]
            W("vector", l_x)
            v_nsum = p.step("vector", lambda E, X=X, py=py: E.reduce_sum(
                out=tcol(py, 0), in_=X[:], axis=AX.X, negate=True), inc="vec")
            W("vector", v_nsum)
            v_nmean = p.step("vector", lambda E, py=py: E.tensor_scalar(
                out=tcol(py, 1), in0=tcol(py, 0), scalar1=1.0 / D, scalar2=None,
                op0=ALU.mult), inc="vec")
            W("scalar", l_x)
            s_ssum = p.step("scalar", lambda E, X=X, py=py: E.activation(
                out=bigB[:, 1024 + py * 1024: 2048 + py * 1024], in_=X[:],
                func=AF.Square, accum_out=tcol(py, 6)), inc="sca")
            W("vector", v_nmean)
            v_m2 = p.step("vector", lambda E, py=py: E.tensor_tensor(
                out=tcol(py, 2), in0=tcol(py, 1), in1=tcol(py, 1), op=ALU.mult), inc="vec")
            W("vector", s_ssum)
            W("vector", v_m2)
            v_var = p.step("vector", lambda E, py=py: E.tensor_scalar(
                out=tcol(py, 3), in0=tcol(py, 6), scalar1=1.0 / D, scalar2=tcol(py, 2),
                op0=ALU.mult, op1=ALU.subtract), inc="vec")
            W("scalar", v_var)
            s_ln = p.step("scalar", lambda E, py=py: E.activation(
                out=tcol(py, 4), in_=tcol(py, 3), func=AF.Ln, bias=epsc[:, 0:1], scale=1.0), inc="sca")
            W("scalar", s_ln)
            s_rstd = p.step("scalar", lambda E, py=py: E.activation(
                out=tcol(py, 4), in_=tcol(py, 4), func=AF.Exp, scale=-0.5), inc="sca")
            W("vector", s_rstd)
            v_xn = p.step("vector", lambda E, X=X, py=py: E.tensor_scalar(
                out=X[:], in0=X[:], scalar1=tcol(py, 1), scalar2=tcol(py, 4),
                op0=ALU.add, op1=ALU.mult), inc="vec")
            v_xn_tok[cc] = v_xn
            W("gpsimd", v_xn)
            xn_spill[cc] = dma("gpsimd", "lg_", XNtok_d[cc * 128:(cc + 1) * 128, :], X[:])

        emit_load(0)
        emit_ln(0)
        emit_load(1)
        emit_ln(1)
        for c in range(C):
            py = c % 2
            X = x_buf[py]
            v_xn = v_xn_tok[c]

            W("tensor", v_xn)
            if c == 0:
                W("tensor", g_const)
            else:
                W("tensor", qk_evac[2])
            t_tr = None
            for k in range(DC):
                t_tr = p.step("tensor", lambda E, X=X, k=k: E.transpose(
                    pb[2 + k // 4][:, (k % 4) * 128:(k % 4) * 128 + 128],
                    X[:, k * 128:(k + 1) * 128], identf[:]), inc="ten")
            x_last_use[py] = t_tr
            if c + 2 < C:
                emit_load(c + 2)
            W("scalar", t_tr)
            v_trev = None
            for h in range(2):
                v_trev = p.step("scalar", lambda E, h=h: E.copy(
                    out=xnT_f[:, h * 512:(h + 1) * 512], in_=pb[2 + h][:, 0:512]), inc="sca")
            tr_evac_prev = v_trev
            W("scalar", v_trev)
            if c >= 2:
                W("scalar", spl_xnt[c - 2])
                W("scalar", k_mm_last)
            s_cast = p.step("scalar", lambda E, py=py: E.copy(
                out=xnTb[:, py * 1024:(py + 1) * 1024], in_=xnT_f[:]), inc="sca")
            s_cast_c[c] = s_cast
            s_cast_last = s_cast

            W("tensor", v_trev)
            if c > 0:
                W("tensor", lg_evac_prev)
            if c == 0:
                W("tensor", l_gw)
                W("tensor", v_bvb)
            t_lgm = None
            for k in range(DC):
                t_lgm = p.step("tensor", lambda E, k=k: E.matmul(
                    pb[6][:, 0:8], xnT_f[:, k * 128:(k + 1) * 128],
                    gwsb[:, k * 8:(k + 1) * 8], start=(k == 0), stop=(k == DC - 1)),
                    inc="ten")
            W("vector", t_lgm)
            v_lg = p.step("vector", lambda E: E.tensor_copy(out=lg[:], in_=pb[6][:, 0:8]),
                          inc="vec")
            lg_evac_prev = v_lg
            W("vector", v_lg)
            v_top = p.step("vector", lambda E, py=py: E.reduce_max(
                out=tcol(py, 7), in_=lg[:], axis=AX.X), inc="vec")
            W("vector", v_top)
            v_eq = p.step("vector", lambda E, py=py: E.tensor_scalar(
                out=lg[:], in0=lg[:], scalar1=tcol(py, 7), scalar2=None,
                op0=ALU.is_equal), inc="vec")
            W("vector", v_eq)
            v_t8 = p.step("vector", lambda E: E.scalar_tensor_tensor(
                out=lg[:], in0=lg[:], scalar=1.0, in1=iotam9[:], op0=ALU.mult,
                op1=ALU.mult), inc="vec")
            W("vector", v_t8)
            v_t9 = p.step("vector", lambda E: E.tensor_scalar(
                out=lg[:], in0=lg[:], scalar1=9.0, scalar2=None, op0=ALU.add), inc="vec")
            W("vector", v_t9)
            v_idx = p.step("vector", lambda E, py=py, c=c: E.tensor_reduce(
                out=sel_all[:, c:c + 1], in_=lg[:], axis=AX.X, op=ALU.min), inc="vec")
            W("scalar", v_top)
            s_e1 = p.step("scalar", lambda E, py=py: E.activation(
                out=tcol(py, 10), in_=tcol(py, 7), func=AF.Exp, scale=-1.0), inc="sca")
            W("scalar", s_e1)
            s_e2 = p.step("scalar", lambda E, py=py: E.add(
                out=tcol(py, 10), in_=tcol(py, 10), add=1.0), inc="sca")
            W("vector", s_e2)
            v_rw = p.step("vector", lambda E, py=py, c=c: E.reciprocal(
                out=rw_all[:, c:c + 1], in_=tcol(py, 10)), inc="vec")
            router_done = v_rw

            xn_bf = xnTb[:, py * 1024:(py + 1) * 1024]
            W("tensor", s_cast)
            W("tensor", v_trev)
            if c == 0:
                W("tensor", g_wk)
            t_k = None
            for qf in range(8):
                for k in range(DC):
                    t_k = p.step("tensor", lambda E, qf=qf, k=k, xn_bf=xn_bf: E.matmul(
                        pb[2 + qf // 4][:, (qf % 4) * 128:(qf % 4) * 128 + 128],
                        wk_sb[:, k * 1024 + qf * 128: k * 1024 + qf * 128 + 128],
                        xn_bf[:, k * 128:(k + 1) * 128],
                        start=(k == 0), stop=(k == DC - 1)), inc="ten")
            k_mm_last = t_k
            W("vector", t_k)
            if c == 0:
                W("vector", v_bb)
            s_ke = None
            for hb in range(2):
                s_ke = p.step("vector", lambda E, hb=hb, c=c: E.tensor_tensor(
                    out=bigA.rearrange("p (k t) -> p k t", k=DC)
                        [:, hb * 4:(hb + 1) * 4, c * 128:(c + 1) * 128],
                    in0=pb[2 + hb][:, 0:512].rearrange("p (q t) -> p q t", q=4),
                    in1=bkb[:, hb * 512:(hb + 1) * 512].rearrange("p (q t) -> p q t", q=4),
                    op=ALU.add), inc="vec")
            qk_evac[2] = qk_evac[3] = s_ke
            if c + 2 < C:
                emit_ln(c + 2)

            if c > 0:
                W("tensor", qk_evac[0])
            if c == 0:
                W("tensor", g_wq)
            t_q = None
            for qf in range(8):
                for k in range(DC):
                    t_q = p.step("tensor", lambda E, qf=qf, k=k, xn_bf=xn_bf: E.matmul(
                        pb[qf // 4][:, (qf % 4) * 128:(qf % 4) * 128 + 128],
                        wq_sb[:, k * 1024 + qf * 128: k * 1024 + qf * 128 + 128],
                        xn_bf[:, k * 128:(k + 1) * 128],
                        start=(k == 0), stop=(k == DC - 1)), inc="ten")
            W("vector", t_q)
            if c == 0:
                W("vector", v_bb)
            if c >= 2:
                W("vector", stats_pair_tok[(c - 2) // 2])
            s_qe = None
            for hb in range(2):
                s_qe = p.step("vector", lambda E, hb=hb, py=py: E.tensor_tensor(
                    out=q_tmp[:, py * 1024 + hb * 512: py * 1024 + hb * 512 + 512],
                    in0=pb[hb][:, 0:512], in1=bqb[:, hb * 512:(hb + 1) * 512],
                    op=ALU.add), inc="vec")
            qk_evac[0] = qk_evac[1] = s_qe
            q_evac[py] = s_qe
            q_mm_last = t_q
            W("gpsimd", s_cast)
            spl_xnt[c] = dma("gpsimd", "lg_",
                XNT_d.rearrange("p (k t) -> p k t", k=DC)[:, :, c * 128:(c + 1) * 128],
                xnTb.rearrange("p (pp k t) -> p pp k t", pp=2, k=DC)[:, py, :, :])

            if py == 1:
                npairs = c + 1
                m_pair = c // 2
                base = m_pair * (m_pair + 1)
                STATS_BANKS = (4, 5, 7)
                W("tensor", q_evac[0])
                W("tensor", q_evac[1])
                W("tensor", qk_evac[2])
                W("tensor", prev_nmu)
                if c == 1:
                    W("vector", v_init)
                for J in range(npairs):
                    pr = J % 3
                    bnk = pb[STATS_BANKS[pr]]
                    if exp_par[pr]:
                        W("tensor", exp_par[pr])
                    t_s = None
                    for half in range(2):
                        for k in range(DC):
                            t_s = p.step("tensor", lambda E, J=J, k=k, half=half, bnk=bnk: E.matmul(
                                bnk[:, half * 128: half * 128 + 128],
                                bigA[:, k * T + J * 128: k * T + J * 128 + 128],
                                q_tmp[:, half * 1024 + k * 128: half * 1024 + k * 128 + 128],
                                start=(k == 0), stop=(k == DC - 1)), inc="ten")
                    stats_mm_last = t_s
                    W("vector", t_s)
                    v_mask = None
                    if J == c - 1:
                        v_mask = p.step("vector", lambda E, bnk=bnk: E.tensor_tensor(
                            out=bnk[:, 0:128], in0=bnk[:, 0:128], in1=cmask[:],
                            op=ALU.add), inc="vec")
                    elif J == c:
                        v_mask = p.step("vector", lambda E, bnk=bnk: E.tensor_scalar(
                            out=bnk[:, 0:128], in0=bnk[:, 0:128], scalar1=-1e30,
                            scalar2=None, op0=ALU.add), inc="vec")
                        W("vector", v_mask)
                        v_mask = p.step("vector", lambda E, bnk=bnk: E.tensor_tensor(
                            out=bnk[:, 128:256], in0=bnk[:, 128:256], in1=cmask[:],
                            op=ALU.add), inc="vec")
                    if v_mask:
                        W("vector", v_mask)
                    v_am = p.step("vector", lambda E, bnk=bnk, pr=pr: E.reduce_max(
                        out=amax[:, pr:pr + 1], in_=bnk[:, 0:256], axis=AX.X,
                        negate=True), inc="vec")
                    W("vector", v_am)
                    v_min = p.step("vector", lambda E, J=J, pr=pr, base=base: E.tensor_scalar(
                        out=cjh[:, base + J:base + J + 1], in0=amax[:, pr:pr + 1],
                        scalar1=nm[:, J:J + 1], scalar2=None, op0=ALU.min), inc="vec")
                    last_vmin = v_min
                    spr = J % 4
                    W("scalar", v_min)
                    W("scalar", sp_slot_prev[spr])
                    s_exp = p.step("scalar", lambda E, J=J, bnk=bnk, spr=spr, base=base: E.activation(
                        out=junk_b[:, spr * 256:(spr + 1) * 256], in_=bnk[:, 0:256],
                        func=AF.Exp, bias=cjh[:, base + J:base + J + 1],
                        accum_out=srow[:, J:J + 1]), inc="sca")
                    exp_par[pr] = s_exp
                    last_sexp = s_exp
                    W("sync", s_exp)
                    sp_tok[base + J] = p.dma("sync",
                        SP_d[:, (base + J) * 256:(base + J) * 256 + 256],
                        junk_b[:, spr * 256:(spr + 1) * 256])
                    sp_slot_prev[spr] = sp_tok[base + J]
                stats_pair_tok[m_pair] = stats_mm_last
                W("vector", last_vmin)
                v_diff = p.step("vector", lambda E, npairs=npairs, base=base: E.tensor_tensor(
                    out=diffb[:, 0:npairs], in0=cjh[:, base:base + npairs],
                    in1=nm[:, 0:npairs], op=ALU.subtract), inc="vec")
                W("vector", v_diff)
                v_nmu = p.step("vector", lambda E, npairs=npairs, base=base: E.tensor_copy(
                    out=nm[:, 0:npairs], in_=cjh[:, base:base + npairs]), inc="vec")
                W("scalar", last_sexp)
                W("scalar", v_diff)
                s_cf = p.step("scalar", lambda E, npairs=npairs: E.activation(
                    out=cfb[:, 0:npairs], in_=diffb[:, 0:npairs], func=AF.Exp), inc="sca")
                sca_p1_last = s_cf
                W("vector", s_cf)
                v_ssu = p.step("vector", lambda E, npairs=npairs: E.tensor_tensor(
                    out=ss[:, 0:npairs], in0=ss[:, 0:npairs], in1=cfb[:, 0:npairs],
                    op=ALU.mult), inc="vec")
                W("vector", v_ssu)
                stats_done = p.step("vector", lambda E, npairs=npairs: E.tensor_tensor(
                    out=ss[:, 0:npairs], in0=ss[:, 0:npairs], in1=srow[:, 0:npairs],
                    op=ALU.add), inc="vec")
                prev_nmu = v_nmu

        W("scalar", stats_done)
        s_lns = p.step("scalar", lambda E: E.activation(
            out=diffb[:, 0:C], in_=ss[:, 0:C], func=AF.Ln, scale=32.0), inc="sca")
        W("vector", s_lns)
        v_nmc = p.step("vector", lambda E: E.tensor_tensor(
            out=nm[:, 0:C], in0=nm[:, 0:C], in1=diffb[:, 0:C], op=ALU.subtract), inc="vec")

        ib = auxf[:, 2048:3072]
        me_t = lambda: ib[:, 0:C]
        scan_t = ib[:, 64:64 + C]
        bo_t = ib[:, 128:128 + C]
        po_t = ib[:, 192:192 + C]
        tmp_t = ib[:, 256:256 + C]
        cnt8 = ib[:, 320:328]
        base8 = ib[:, 328:336]
        slotf = ib[:, 384:384 + C]
        tokid_i = sb("tokid_i", [128, C], mybir.dt.int32)
        slot_i = sb("slot_i", [128, C], mybir.dt.int32)
        pre4 = ib[:, 832:832 + (NSLOT // 128) * 4].bitcast(mybir.dt.int32)
        zrow = sb("zrow", [1, D], BF16)

        g_tok = p.step("gpsimd", lambda E: E.iota(tokid_i[:], pattern=[[128, C]], base=0,
               channel_multiplier=1, allow_small_or_imprecise_dtypes=True), inc="gp")
        W("gpsimd", qk_evac[2])
        g_pre = p.step("gpsimd", lambda E: E.memset(pre4[:], T), inc="gp")
        pre_s = sb("pre_s", [128, NSLOT // 128], mybir.dt.int32)
        g_ps1 = p.step("gpsimd", lambda E: E.iota(pre_s[:], pattern=[[1, NSLOT // 128]],
               base=0, channel_multiplier=NSLOT // 128,
               allow_small_or_imprecise_dtypes=True), inc="gp")
        W("vector", router_done)
        W("vector", g_ps1)
        v_ps = p.step("vector", lambda E: E.tensor_scalar(
            out=pre_s[:], in0=pre_s[:], scalar1=255, scalar2=None,
            op0=ALU.bitwise_and), inc="vec")
        W("vector", v_ps)
        v_ps = p.step("vector", lambda E: E.tensor_scalar(
            out=pre_s[:], in0=pre_s[:], scalar1=T, scalar2=None,
            op0=ALU.add), inc="vec")
        pre4v = pre4.rearrange("p (f j) -> p f j", j=4)
        W("vector", g_pre)
        W("vector", v_ps)
        v_p41 = p.step("vector", lambda E: E.tensor_copy(
            out=pre4v[:, :, 1], in_=pre_s[:]), inc="vec")
        v_p42 = p.step("vector", lambda E: E.tensor_scalar(
            out=pre4v[:, :, 2], in0=pre4v[:, :, 2], scalar1=0, scalar2=None,
            op0=ALU.bitwise_and), inc="vec")
        g_zr = p.step("gpsimd", lambda E: E.memset(zrow[:], 0.0), inc="gp")
        W("gpsimd", g_zr)
        W("gpsimd", v_p41)
        W("gpsimd", v_p42)
        l_pre1 = dma("gpsimd", "lg_", TBL_d.rearrange("(p f) j -> p (f j)", p=128),
                     pre4[:])
        l_zr = dma("gpsimd", "lg_", XNtok_d[T:T + 1, :], zrow[:])

        W("vector", router_done)
        W("vector", qk_evac[2])
        v_bo = p.step("vector", lambda E: E.memset(bo_t[:], 0.0), inc="vec")
        v_po = p.step("vector", lambda E: E.memset(po_t[:], 0.0), inc="vec")
        v_c8 = p.step("vector", lambda E: E.memset(cnt8[:], 0.0), inc="vec")
        scan_done = [None] * 8
        for e in range(8):
            me = me_t()
            W("vector", v_c8)
            v_me = p.step("vector", lambda E, e=e: E.tensor_scalar(
                out=me[:], in0=sel_all[:, 0:C], scalar1=float(e), scalar2=None,
                op0=ALU.is_equal), inc="vec")
            W("vector", v_me)
            v_sc = p.step("vector", lambda E: E.tensor_tensor_scan(
                out=scan_t[:], data0=me[:], data1=me[:], initial=0.0,
                op0=ALU.add, op1=ALU.bypass), inc="vec")
            W("vector", v_sc)
            v_cn = p.step("vector", lambda E, e=e: E.tensor_copy(
                out=cnt8[:, e:e + 1], in_=scan_t[:, C - 1:C]), inc="vec")
            W("vector", v_cn)
            v_p1 = p.step("vector", lambda E: E.tensor_tensor(
                out=tmp_t[:], in0=me[:], in1=scan_t[:], op=ALU.mult), inc="vec")
            W("vector", v_p1)
            v_po = p.step("vector", lambda E: E.tensor_tensor(
                out=po_t[:], in0=po_t[:], in1=tmp_t[:], op=ALU.add), inc="vec")
            scan_done[e] = v_po
        W("tensor", lg_evac_prev)
        W("tensor", scan_done[7])
        t_tc = p.step("tensor", lambda E: E.matmul(
            pb[6][0:8, 0:128], cnt8[:, 0:8], identf[:, 0:128],
            start=True, stop=True), inc="ten")
        W("vector", t_tc)
        v_ct = p.step("vector", lambda E: E.tensor_copy(
            out=ib[:, 448:576][0:8, :], in_=pb[6][0:8, 0:128]), inc="vec")
        W("vector", v_ct)
        v_sc8 = p.step("vector", lambda E: E.tensor_tensor_scan(
            out=ib[:, 576:704][0:8, :], data0=ib[:, 448:576][0:8, :],
            data1=ib[:, 448:576][0:8, :], initial=0.0,
            op0=ALU.add, op1=ALU.bypass), inc="vec")
        W("vector", v_sc8)
        v_ex8 = p.step("vector", lambda E: E.tensor_tensor(
            out=ib[:, 576:704][0:8, :], in0=ib[:, 576:704][0:8, :],
            in1=ib[:, 448:576][0:8, :], op=ALU.subtract), inc="vec")
        W("tensor", v_ex8)
        t_tb = p.step("tensor", lambda E: E.matmul(
            pb[6][:, 128:136], ib[:, 576:704][0:8, :], identf[0:8, 0:8],
            start=True, stop=True), inc="ten")
        W("vector", t_tb)
        v_b8 = p.step("vector", lambda E: E.tensor_copy(
            out=base8[:], in_=pb[6][:, 128:136]), inc="vec")
        lg_evac_prev = v_b8
        for e in range(8):
            me = me_t()
            W("vector", v_b8)
            v_me = p.step("vector", lambda E, e=e: E.tensor_scalar(
                out=me[:], in0=sel_all[:, 0:C], scalar1=float(e), scalar2=None,
                op0=ALU.is_equal), inc="vec")
            W("vector", v_me)
            v_t = p.step("vector", lambda E, e=e: E.tensor_scalar(
                out=tmp_t[:], in0=me[:], scalar1=base8[:, e:e + 1], scalar2=None,
                op0=ALU.mult), inc="vec")
            W("vector", v_t)
            v_bo = p.step("vector", lambda E: E.tensor_tensor(
                out=bo_t[:], in0=bo_t[:], in1=tmp_t[:], op=ALU.add), inc="vec")
        W("vector", v_bo)
        v_s1 = p.step("vector", lambda E: E.tensor_scalar(
            out=slotf[:], in0=sel_all[:, 0:C], scalar1=float(CAP), scalar2=None,
            op0=ALU.mult), inc="vec")
        W("vector", v_s1)
        v_s2 = p.step("vector", lambda E: E.tensor_tensor(
            out=slotf[:], in0=slotf[:], in1=bo_t[:], op=ALU.add), inc="vec")
        W("vector", v_s2)
        v_s3 = p.step("vector", lambda E: E.scalar_tensor_tensor(
            out=slotf[:], in0=slotf[:], scalar=-1.0, in1=po_t[:],
            op0=ALU.add, op1=ALU.add), inc="vec")
        W("vector", v_s3)
        v_s4 = p.step("vector", lambda E: E.tensor_scalar(
            out=slotf[:], in0=slotf[:], scalar1=float(NSLOT - 1), scalar2=None,
            op0=ALU.min), inc="vec")
        W("vector", v_s4)
        v_si = p.step("vector", lambda E: E.tensor_copy(out=slot_i[:], in_=slotf[:]),
                      inc="vec")
        combf = ib[:, 704:832]
        combi = combf.bitcast(mybir.dt.int32)
        combiv = combi.rearrange("p (c j) -> p c j", j=4)
        W("vector", g_tok)
        W("vector", router_done)
        v_cb0 = p.step("vector", lambda E: E.tensor_copy(
            out=combiv[:, :, 0], in_=tokid_i[:]), inc="vec")
        v_cb1 = p.step("vector", lambda E: E.tensor_copy(
            out=combiv[:, :, 1], in_=tokid_i[:]), inc="vec")
        v_cb2 = p.step("vector", lambda E: E.tensor_copy(
            out=combf.rearrange("p (c j) -> p c j", j=4)[:, :, 2],
            in_=rw_all[:, 0:C]), inc="vec")
        W("gpsimd", v_si)
        W("gpsimd", v_cb0)
        W("gpsimd", v_cb1)
        W("gpsimd", v_cb2)
        W("gpsimd", l_pre1)
        from concourse.bass_core_rust import IndirectOffsetOnAxis
        sc_tok = sc_prev = None
        for cc in range(C):
            sc_prev = sc_tok
            sc_tok = p.dma_raw("gpsimd", lambda E, cc=cc: E.indirect_dma_start(
                out=TBL_d[:, :], out_offset=IndirectOffsetOnAxis(slot_i[:, cc:cc + 1], 0),
                in_=combi[:, cc * 4:cc * 4 + 4], in_offset=None))
        scatters_done = (sc_prev, sc_tok)

        xv_use = [0, 0]
        vt_done = [0] * C
        for t in range(C):
            py = t % 2
            W("sync", spl_xnt[t])
            if t < 2:
                W("sync", k_mm_last)
                W("sync", spl_xnt[C - 2])
                W("sync", spl_xnt[C - 1])
            else:
                W("sync", xv_use[py])
            l_xv = dma("sync", "ls",
                xnTb.rearrange("p (pp k t) -> p pp k t", pp=2, k=DC)[:, py, :, :],
                XNT_d.rearrange("p (k t) -> p k t", k=DC)[:, :, t * 128:(t + 1) * 128])
            W("tensor", l_xv)
            if t == 0:
                W("tensor", g_wv)
                W("tensor", q_evac[0])
                W("tensor", q_evac[1])
            if t == 1:
                W("tensor", qk_evac[2])
            if t >= 2:
                W("tensor", vt_done[t - 2])
            bb = 2 * (t % 2)
            t_v = None
            for h in range(2):
                for k in range(DC):
                    t_v = p.step("tensor", lambda E, h=h, k=k, py=py, bb=bb: E.matmul(
                        pb[bb + h][:, 0:512],
                        xnTb[:, py * 1024 + k * 128: py * 1024 + k * 128 + 128],
                        wv_sb[:, k * 1024 + h * 512: k * 1024 + h * 512 + 512],
                        start=(k == 0), stop=(k == DC - 1)), inc="ten")
            xv_use[py] = t_v
            W("vector", t_v)
            if t == 0:
                W("vector", v_bvb)
                W("vector", sca_p1_last)
                W("vector", last_sexp)
                for _pr in range(4):
                    W("vector", sp_slot_prev[_pr])
                W("vector", xn_spill[C - 1])
                W("vector", xn_spill[C - 2])
            v_vt = None
            for h in range(2):
                v_vt = p.step("vector", lambda E, h=h, t=t, bb=bb: E.tensor_tensor(
                    out=bigB[:, t * 1024 + h * 512: t * 1024 + h * 512 + 512],
                    in0=pb[bb + h][:, 0:512], in1=bvb[:, h * 512:(h + 1) * 512],
                    op=ALU.add), inc="vec")
            vt_done[t] = v_vt
        xv_last = xv_use[(C - 1) % 2]

        pt_buf = [q_tmp[:, i * 256:(i + 1) * 256] for i in range(8)]
        rdev = [wsb[:, 1024 + 1024 * i: 1024 + 1024 * (i + 1)] for i in range(2)]

        def expert_w(par):
            base = 12288 * (1 - par)
            return (wsb[:, base: base + 4096], wsb[:, base + 4096: base + 8192],
                    wsb[:, base + 8192: base + 12288])

        W("gpsimd", xv_last)
        wg0, wu0, wd0 = expert_w(0)
        g_we = [None] * E
        wg_t = dma("gpsimd", "lg_", wg0.rearrange("p (k f) -> p k f", k=DC),
                   wg_d[0].rearrange("(k p) f -> p k f", p=128))
        wu_t = dma("gpsimd", "lg_", wu0.rearrange("p (k f) -> p k f", k=DC),
                   wu_d[0].rearrange("(k p) f -> p k f", p=128))
        wd_t = dma("gpsimd", "lg_", wd0.rearrange("p (q f) -> p q f", q=IC),
                   wd_d[0].rearrange("(q p) f -> p q f", p=128))
        g_we[0] = (wg_t, wu_t, wd_t)

        pt_used = [0] * 8
        acc_evac = [0, 0]
        rd_spill = [0] * C
        rdev_free = [0, 0]
        last_pv = 0
        s_cf = 0
        v_pt = 0
        pair_i = 0
        for g in range(C // 2):
            t0, t1 = 2 * g, 2 * g + 1
            n = t1 + 1
            base = g * (g + 1)
            W("vector", v_nmc)
            W("vector", s_cf)
            v_df2 = p.step("vector", lambda E, n=n, base=base: E.tensor_tensor(
                out=diffb[:, 0:n], in0=nm[:, 0:n], in1=cjh[:, base:base + n],
                op=ALU.subtract), inc="vec")
            W("scalar", v_df2)
            W("scalar", v_pt)
            s_cf = p.step("scalar", lambda E, n=n: E.activation(
                out=cfP[:, 0:n], in_=diffb[:, 0:n], func=AF.Exp), inc="sca")
            l_tok = [0] * n
            for J in range(n):
                pr = (pair_i + J) % 8
                if l_tok[J] == 0:
                    wide = (J + 1 < n) and (pr != 7)
                    W("sync", pt_used[pr])
                    if wide:
                        W("sync", pt_used[pr + 1])
                    if base + J < 8:
                        W("sync", stats_mm_last)
                    W("sync", sp_tok[base + J])
                    if wide:
                        W("sync", sp_tok[base + J + 1])
                        l_tok[J] = l_tok[J + 1] = p.dma(
                            "sync", q_tmp[:, pr * 256:(pr + 2) * 256],
                            SP_d[:, (base + J) * 256:(base + J + 2) * 256])
                    else:
                        l_tok[J] = p.dma("sync", pt_buf[pr][:, 0:256],
                            SP_d[:, (base + J) * 256:(base + J) * 256 + 256])
                W("vector", l_tok[J])
                W("vector", s_cf)
                v_pt = p.step("vector", lambda E, J=J, pr=pr: E.tensor_scalar(
                    out=pt_buf[pr][:, 0:256], in0=pt_buf[pr][:, 0:256],
                    scalar1=cfP[:, J:J + 1], scalar2=None, op0=ALU.mult), inc="vec")
                W("tensor", v_pt)
                if J == 0:
                    W("tensor", vt_done[t1])
                    W("tensor", acc_evac[0])
                    W("tensor", acc_evac[1])
                for t in (t0, t1):
                    if J > t:
                        continue
                    pcol = (t - t0) * 128
                    for h in range(2):
                        last_pv = p.step("tensor", lambda E, t=t, J=J, h=h, pcol=pcol, pr=pr: E.matmul(
                            pb[(t % 2) * 2 + h][:, 0:512],
                            pt_buf[pr][:, pcol: pcol + 128],
                            bigB[:, J * 1024 + h * 512: J * 1024 + h * 512 + 512],
                            start=(J == 0), stop=(J == t)), inc="ten")
                    pt_used[pr] = last_pv
            pair_i += n
            for t in (t0, t1):
                W("vector", last_pv)
                W("vector", rdev_free[t % 2])
                v_rd = None
                for h in range(2):
                    v_rd = p.step("vector", lambda E, t=t, h=h: E.tensor_copy(
                        out=rdev[t % 2][:, h * 512:(h + 1) * 512],
                        in_=pb[(t % 2) * 2 + h][:, 0:512]), inc="vec")
                acc_evac[t % 2] = v_rd
                W("scalar", v_rd)
                rd_spill[t] = dma("scalar", "lo", RD_d[t * 128:(t + 1) * 128, :],
                                  rdev[t % 2][:])
                rdev_free[t % 2] = rd_spill[t]

        SPE = CAP // 128
        assert SPE >= 4
        S = 8 * SPE
        xg_of = lambda u: bigB[:, (u % 8) * 1024: (u % 8) * 1024 + 1024]
        xgT_of = lambda u: bigB[:, 8192 + (u % 2) * 1024: 8192 + (u % 2) * 1024 + 1024]
        out_of = lambda u: bigB[:, 10240 + (u % 2) * 1024: 10240 + (u % 2) * 1024 + 1024]
        tbl_sb = sb("tbl_sb", [128, 64], mybir.dt.int32)
        tbl_fb = tbl_sb.bitcast(F32)

        def G_of(u):
            return auxf[:, (u % 2) * 1536: (u % 2) * 1536 + 512]

        def A_of(u):
            return auxf[:, (u % 2) * 1536 + 512: (u % 2) * 1536 + 1024]

        def SIL_of(u):
            return auxf[:, (u % 2) * 1536 + 1024: (u % 2) * 1536 + 1536]

        def h_of(u):
            return xnTb[:, (u % 2) * 512: (u % 2) * 512 + 512]

        def ht_of(u):
            return xnTb[:, 1024 + (u % 2) * 512: 1024 + (u % 2) * 512 + 512]

        tb_tok = [0] * S
        xsc = [bigB[:, 12288 + 2048 * i: 12288 + 2048 * (i + 1)].bitcast(F32) for i in range(2)]
        rsc = [bigB[:, 16384 + 1024 * i: 16384 + 1024 * (i + 1)] for i in range(2)]
        p4a_done = [0] * C
        p4a_free = [0, 0]
        p4a_t = 0
        gx_tok = [0] * S
        xgT_ev = [0] * S
        t_xgT = [0] * S
        t_gu_tok = [0] * S
        v_h_tok = [0] * S
        ht_ev = [0] * S
        ht_cons = [0] * S
        out_ev = [0] * S
        out_sc = [0] * S
        gu_free = [acc_evac[0], acc_evac[1]]
        ta_free = [last_sexp, last_sexp]
        htp_free = lg_evac_prev
        down_free = exp_par[2]
        h_free = [0, 0]
        aux_free = [0, 0]
        exp_done = [0] * E
        last_acc_add = 0
        last_sc = 0

        for u in range(S + 5):
            vs = [0, 1, 2, 3, 4, 5] if u == 0 else [u + 5]
            for v in vs:
                if v >= S:
                    continue
                if v >= 16:
                    W("sync", out_sc[v - 16])
                if v < 4:
                    for sct in scatters_done:
                        W("sync", sct)
                tb_tok[v] = p.dma("sync", tbl_sb[:, (v % 16) * 4: (v % 16) * 4 + 4],
                                  TBL_d[v * 128:(v + 1) * 128, :])
            gvs = [0, 1, 2, 3, 4] if u == 0 else [u + 4]
            for v in gvs:
                if v >= S:
                    continue
                W("gpsimd", tb_tok[v])
                if v < 5:
                    W("gpsimd", last_pv)
                    W("gpsimd", l_zr)
                else:
                    W("gpsimd", t_xgT[v - 5])
                gx_tok[v] = p.dma_raw("gpsimd", lambda E, v=v: E.indirect_dma_start(
                    out=xg_of(v)[:, :], out_offset=None, in_=XNtok_d[:, :],
                    in_offset=IndirectOffsetOnAxis(
                        tbl_sb[:, (v % 16) * 4: (v % 16) * 4 + 1], 0)))

            ub = u - 3
            if 0 <= ub < S:
                W("tensor", v_h_tok[ub])
                W("tensor", htp_free)
                htp = pb[6][:, 0:512].bitcast(BF16)
                t_ht = None
                for qi in range(IC):
                    t_ht = p.step("tensor", lambda E_, qi=qi, htp=htp, ub=ub: E_.transpose(
                        htp[:, qi * 128:(qi + 1) * 128],
                        h_of(ub)[:, qi * 128:(qi + 1) * 128], identb[:]), inc="ten")
                h_free[ub % 2] = t_ht
                W("scalar", t_ht)
                if ub >= 2:
                    W("scalar", ht_cons[ub - 2])
                s_ht = p.step("scalar", lambda E_, htp=htp, ub=ub: E_.copy(
                    out=ht_of(ub), in_=htp[:, 0:512]), inc="sca")
                ht_ev[ub] = s_ht
                htp_free = s_ht

            uc = u - 4
            if 0 <= uc < S:
                ec = uc // SPE
                wde = expert_w(ec % 2)[2]
                W("tensor", ht_ev[uc])
                t_d = None
                for hh in range(2):
                    W("tensor", down_free)
                    for qi in range(IC):
                        t_d = p.step("tensor", lambda E_, qi=qi, hh=hh, wde=wde, uc=uc: E_.matmul(
                            pb[7][:, 0:512],
                            ht_of(uc)[:, qi * 128:(qi + 1) * 128],
                            wde[:, qi * 1024 + hh * 512: qi * 1024 + hh * 512 + 512],
                            start=(qi == 0), stop=(qi == IC - 1)), inc="ten")
                    W("vector", t_d)
                    if uc >= 2:
                        W("vector", out_sc[uc - 2])
                    last_acc_add = p.step("vector", lambda E_, uc=uc, hh=hh: E_.tensor_copy(
                        out=out_of(uc)[:, hh * 512:(hh + 1) * 512], in_=pb[7][:, 0:512]),
                        inc="vec")
                    down_free = last_acc_add
                ht_cons[uc] = t_d
                if uc == (ec + 1) * SPE - 1:
                    exp_done[ec] = t_d
                out_ev[uc] = last_acc_add
                W("gpsimd", last_acc_add)
                W("gpsimd", tb_tok[uc])
                out_sc[uc] = p.dma_raw("gpsimd", lambda E, uc=uc: E.indirect_dma_start(
                    out=FFNS_d[:, :], out_offset=IndirectOffsetOnAxis(
                        tbl_sb[:, (uc % 16) * 4 + 1: (uc % 16) * 4 + 2], 0),
                    in_=out_of(uc)[:, :], in_offset=None))
                last_sc = out_sc[uc]

            if u < S:
                e, t = divmod(u, SPE)
                wge, wue, wde_ = expert_w(e % 2)
                if e + 1 < E and t == 3:
                    wgn, wun, wdn = expert_w((e + 1) % 2)
                    if e == 0:
                        W("gpsimd", last_pv)
                        W("gpsimd", acc_evac[0])
                        W("gpsimd", acc_evac[1])
                        W("gpsimd", rdev_free[0])
                        W("gpsimd", rdev_free[1])
                    else:
                        W("gpsimd", exp_done[e - 1])
                    wg_t = dma("gpsimd", "lg_", wgn.rearrange("p (k f) -> p k f", k=DC),
                               wg_d[e + 1].rearrange("(k p) f -> p k f", p=128))
                    wu_t = dma("gpsimd", "lg_", wun.rearrange("p (k f) -> p k f", k=DC),
                               wu_d[e + 1].rearrange("(k p) f -> p k f", p=128))
                    wd_t = dma("gpsimd", "lg_", wdn.rearrange("p (q f) -> p q f", q=IC),
                               wd_d[e + 1].rearrange("(q p) f -> p q f", p=128))
                    g_we[e + 1] = (wg_t, wu_t, wd_t)
                W("tensor", gx_tok[u])
                W("tensor", ta_free[u % 2])
                tap = pb[4 + u % 2][:, 0:512].bitcast(BF16)
                t_tx = None
                for k in range(DC):
                    t_tx = p.step("tensor", lambda E_, k=k, tap=tap, u=u: E_.transpose(
                        tap[:, k * 128:(k + 1) * 128],
                        xg_of(u)[:, k * 128:(k + 1) * 128], identb[:]), inc="ten")
                t_xgT[u] = t_tx
                W("scalar", t_tx)
                if u >= 2:
                    W("scalar", t_gu_tok[u - 2])
                s_xe = p.step("scalar", lambda E_, u=u: E_.copy(
                    out=xgT_of(u), in_=pb[4 + u % 2][:, 0:512].bitcast(BF16)[:, 0:1024]),
                    inc="sca")
                xgT_ev[u] = s_xe
                ta_free[u % 2] = s_xe

            if p4a_t < C and u >= 2 and (u % 3 != 2 or S + 6 - u <= C - p4a_t + 2):
                tt = p4a_t
                q4 = tt % 2
                if tt < 2:
                    W("sync", last_pv)
                    W("sync", stats_mm_last)
                else:
                    W("sync", p4a_done[tt - 2])
                W("sync", rd_spill[tt])
                W("sync", xn_spill[C - 1])
                W("sync", xn_spill[C - 2])
                l1 = p.dma("sync", xsc[q4][:, :], x_d[tt * 128:(tt + 1) * 128, :])
                l2 = p.dma("sync", bigA[:, tt * 1024:(tt + 1) * 1024],
                           XNtok_d[tt * 128:(tt + 1) * 128, :])
                l3 = p.dma("sync", rsc[q4][:, :], RD_d[tt * 128:(tt + 1) * 128, :])
                W("vector", l1)
                W("vector", l2)
                v_a1 = p.step("vector", lambda E_, tt=tt, q4=q4: E_.tensor_tensor(
                    out=bigA[:, tt * 1024:(tt + 1) * 1024],
                    in0=bigA[:, tt * 1024:(tt + 1) * 1024], in1=xsc[q4][:, :],
                    op=ALU.add), inc="vec")
                W("vector", l3)
                W("vector", v_a1)
                v_a2 = p.step("vector", lambda E_, tt=tt, q4=q4: E_.tensor_tensor(
                    out=bigA[:, tt * 1024:(tt + 1) * 1024],
                    in0=bigA[:, tt * 1024:(tt + 1) * 1024], in1=rsc[q4][:, :],
                    op=ALU.add), inc="vec")
                p4a_done[tt] = v_a2
                p4a_free[q4] = v_a2
                p4a_t += 1

            ua = u - 1
            if 0 <= ua < S:
                e, t = divmod(ua, SPE)
                wge, wue, _ = expert_w(e % 2)
                sp = ua % 2
                if ua == 0:
                    for wt in g_we[0]:
                        W("tensor", wt)
                elif ua % SPE == 0:
                    for wt in g_we[ua // SPE]:
                        W("tensor", wt)
                W("tensor", xgT_ev[ua])
                W("tensor", gu_free[sp])
                t_gu = None
                for k in range(DC):
                    p.step("tensor", lambda E_, k=k, sp=sp, wge=wge, ua=ua: E_.matmul(
                        pb[sp * 2][:, 0:512],
                        xgT_of(ua)[:, k * 128:(k + 1) * 128],
                        wge[:, k * 512:(k + 1) * 512],
                        start=(k == 0), stop=(k == DC - 1)), inc="ten")
                    t_gu = p.step("tensor", lambda E_, k=k, sp=sp, wue=wue, ua=ua: E_.matmul(
                        pb[sp * 2 + 1][:, 0:512],
                        xgT_of(ua)[:, k * 128:(k + 1) * 128],
                        wue[:, k * 512:(k + 1) * 512],
                        start=(k == 0), stop=(k == DC - 1)), inc="ten")
                t_gu_tok[ua] = t_gu
                sE = tbl_fb[:, (ua % 16) * 4 + 2: (ua % 16) * 4 + 3]
                W("vector", t_gu)
                W("vector", tb_tok[ua])
                W("vector", aux_free[sp])
                v_G = p.step("vector", lambda E_, sE=sE, sp=sp, ua=ua: E_.tensor_scalar(
                    out=G_of(ua), in0=pb[sp * 2][:, 0:512], scalar1=sE, scalar2=None,
                    op0=ALU.mult), inc="vec")
                W("scalar", v_G)
                s_sil = p.step("scalar", lambda E_, ua=ua: E_.activation(
                    out=SIL_of(ua), in_=G_of(ua), func=AF.Sigmoid), inc="sca")
                W("vector", v_G)
                v_A = p.step("vector", lambda E_, sE=sE, sp=sp, ua=ua: E_.scalar_tensor_tensor(
                    out=A_of(ua), in0=pb[sp * 2 + 1][:, 0:512], scalar=sE, in1=G_of(ua),
                    op0=ALU.mult, op1=ALU.mult), inc="vec")
                gu_free[sp] = v_A
                W("vector", s_sil)
                W("vector", v_A)
                W("vector", h_free[sp])
                v_h = p.step("vector", lambda E_, ua=ua: E_.tensor_tensor(
                    out=h_of(ua), in0=A_of(ua), in1=SIL_of(ua), op=ALU.mult), inc="vec")
                v_h_tok[ua] = v_h
                aux_free[sp] = v_h
        down_evac_prev = last_acc_add

        fff = [wsb[:, 1024 * i: 1024 * (i + 1)] for i in range(2)]
        fo = [wsb[:, 20480 + 2048 * i: 20480 + 2048 * (i + 1)].bitcast(F32) for i in range(2)]
        o_reads = [0, 0]
        o_store = [0, 0]
        st_last = 0
        for t in range(C):
            tp = t % 2
            if t == 0:
                W("sync", exp_done[6])
                W("sync", exp_done[E - 1])
                W("sync", out_sc[S - 1])
                W("sync", out_sc[S - 2])
            if t >= 2:
                W("sync", o_reads[tp])
            l4 = dma("sync", "ls", fff[tp][:], FFNS_d[t * 128:(t + 1) * 128, :])
            W("vector", l4)
            W("vector", p4a_done[t])
            W("vector", o_store[tp])
            v_o3 = p.step("vector", lambda E_, tp=tp, t=t: E_.tensor_tensor(
                out=fo[tp][:], in0=bigA[:, t * 1024:(t + 1) * 1024], in1=fff[tp][:],
                op=ALU.add), inc="vec")
            o_reads[tp] = v_o3
            W("scalar", v_o3)
            st_last = dma("scalar", "lo", out_d[t * 128:(t + 1) * 128, :], fo[tp][:])
            o_store[tp] = st_last
        W("scalar", st_last)

        with nc.Block() as block:
            for eng in ENGS:
                getattr(block, eng)(lambda E, eng=eng: [f(E) for f in p.q[eng]])
    return nc


def prep_inputs(x_full, Wq, bq, Wk, bk, Wv, bv, gate_w, w_gate, w_up, w_down):
    B = x_full.shape[0]
    c = np.ascontiguousarray
    shared = {
        "Wq": c(Wq), "Wk": c(Wk), "Wv": c(Wv),
        "bq2": c(bq.reshape(8, 128).T), "bk2": c(bk.reshape(8, 128).T),
        "bvrow": c(bv.reshape(1, 1024)),
        "gwT": c(gate_w.T),
        "wgT": c(np.transpose(w_gate, (0, 2, 1))),
        "wuT": c(np.transpose(w_up, (0, 2, 1))),
        "wdT": c(np.transpose(w_down, (0, 2, 1))),
    }
    return [dict(shared, x=c(x_full[b])) for b in range(B)]


def kernel(**inputs):
    x = np.asarray(inputs["L0_attn_minibatch"])
    in_maps = prep_inputs(
        x, np.asarray(inputs["L0_attn_Wq"]), np.asarray(inputs["L0_attn_bq"]),
        np.asarray(inputs["L0_attn_Wk"]), np.asarray(inputs["L0_attn_bk"]),
        np.asarray(inputs["L0_attn_Wv"]), np.asarray(inputs["L0_attn_bv"]),
        np.asarray(inputs["L0_ffn_gate_w"]), np.asarray(inputs["L0_ffn_w_gate"]),
        np.asarray(inputs["L0_ffn_w_up"]), np.asarray(inputs["L0_ffn_w_down"]))
    nc = build(T=x.shape[1])
    res = run_bass_kernel_spmd(nc, in_maps, core_ids=list(range(x.shape[0])))
    return np.stack([r["out"] for r in res.results], axis=0)
